# version 01
# baseline (speedup 1.0000x reference)
import numpy as np
from contextlib import ExitStack
import concourse.bass as bass
import concourse.mybir as mybir
from concourse.bass_utils import run_bass_kernel_spmd

F32 = mybir.dt.float32
BF16 = mybir.dt.bfloat16
AF = mybir.ActivationFunctionType
ALU = mybir.AluOpType
AX = mybir.AxisListType

D_MODEL = 1024
T_CORE = 2048
NCH = D_MODEL // 128
LN_EPS = 1e-5

ENGINES = ("sync", "gpsimd", "tensor", "vector", "scalar")


class Prog:

    def __init__(self, nc, ctx):
        self.nc = nc
        self.ctx = ctx
        self.ops = {e: [] for e in ENGINES}
        self.cnt = {}
        self.sems = {}

    def _sem(self, name):
        if name not in self.sems:
            self.sems[name] = self.ctx.enter_context(self.nc.semaphore(name))
            self.cnt.setdefault(name, 0)
        return self.sems[name]

    def op(self, eng, fn, waits=(), inc=None, amt=1):
        w = []
        for item in waits:
            if item is None:
                continue
            s, v = item
            if v is None or v <= 0:
                continue
            w.append((s, v))
        ev = None
        if inc is not None:
            self._sem(inc)
            self.cnt[inc] = self.cnt.get(inc, 0) + amt
            ev = (inc, self.cnt[inc])
        self.ops[eng].append((tuple(w), fn, inc))
        return ev

    def dma(self, eng, out, in_, waits=(), ch="in"):
        return self.op(eng, lambda e, o=out, i=in_: e.dma_start(out=o, in_=i),
                       waits=waits, inc="d_" + ch, amt=16)

    def total(self, ch):
        return ("d_" + ch, self.cnt.get("d_" + ch, 0))

    def finish(self, block):
        ws = [(s, self.cnt[s]) for s in self.cnt if s.startswith("d_")]
        self.op("gpsimd", lambda g: g.engine_nop(), waits=ws, inc="sG")
        sems = self.sems
        for eng in ENGINES:
            ops = self.ops[eng]
            if not ops:
                continue

            def make(ops=ops):
                def body(e):
                    hiwater = {}
                    for waits, fn, inc in ops:
                        for s, v in waits:
                            if hiwater.get(s, 0) >= v:
                                continue
                            e.wait_ge(sems[s], v)
                            hiwater[s] = v
                        r = fn(e)
                        if inc is not None:
                            r.then_inc(sems[inc], 16 if inc.startswith("d_") else 1)
                return body
            getattr(block, eng)(make())


class ConstPool:

    def __init__(self, nc, ctx, p, n=16):
        self.tile = ctx.enter_context(nc.sbuf_tensor("cpool", [128, n], F32))[:, :]
        self.nc, self.p = nc, p
        self.vals = {}
        self.k = 0

    def get(self, val):
        val = float(val)
        if val in self.vals:
            return self.vals[val]
        col = self.tile[:, self.k:self.k + 1]
        self.k += 1
        ev = self.p.op("vector", lambda v, col=col, val=val: v.memset(col, val), inc="sV")
        self.nc.const_aps.register(F32, val, col)
        self.vals[val] = ev
        return ev


def emit_ln(p, pool, banks, X, Xn, sq2, xnt, ones_col, ones_row, stats, T, ev_x,
            bank_free_evs=()):
    b_sum, b_sq, b_r, b_m = banks
    done = []
    prev_bc = list(bank_free_evs)
    carry = {}
    for ti in range(T // 512):
        t0 = ti * 512
        ev_sqmm = dict(carry)
        ev_summ = None
        for c in range(NCH):
            xc = X[:, c * T + t0: c * T + t0 + 512]
            ev_summ = p.op("tensor", lambda t, xc=xc, c=c: t.matmul(
                pool[b_sum][0:1, :], ones_col, xc,
                start=(c == 0), stop=(c == NCH - 1)),
                waits=list(ev_x) + (prev_bc if c == 0 else []), inc="sT")
            sq = sq2[:, (c % 2) * 512:(c % 2) * 512 + 512]
            evq = p.op("scalar", lambda s, xc=xc, sq=sq: s.activation(sq, xc, AF.Square),
                       waits=list(ev_x) + ([ev_sqmm.get(c - 2)] if c >= 2 else []),
                       inc="sS")
            ev_sqmm[c] = p.op("tensor", lambda t, c=c, sq=sq: t.matmul(
                pool[b_sq][0:1, :], ones_col, sq,
                start=(c == 0), stop=(c == NCH - 1)),
                waits=[evq], inc="sT")
        ev_sums = ev_sqmm[NCH - 1]
        carry = {-2: ev_sqmm[NCH - 2], -1: ev_sqmm[NCH - 1]}
        mean, var = stats[0:1, 0:512], stats[0:1, 512:1024]
        rstd, ms = stats[0:1, 1024:1536], stats[0:1, 1536:2048]
        e1 = p.op("scalar", lambda s: s.mul(mean, pool[b_sum][0:1, :], 1.0 / D_MODEL),
                  waits=[ev_summ, ev_sums], inc="sS")
        e2 = p.op("scalar", lambda s: s.mul(var, pool[b_sq][0:1, :], 1.0 / D_MODEL),
                  waits=[e1], inc="sS")
        e3 = p.op("vector", lambda v: v.tensor_tensor(ms, mean, mean, ALU.mult),
                  waits=[e1], inc="sV")
        e4 = p.op("vector", lambda v: v.tensor_tensor(var, var, ms, ALU.subtract),
                  waits=[e2, e3], inc="sV")
        e5 = p.op("scalar", lambda s: s.activation(rstd, var, AF.Ln, bias=LN_EPS),
                  waits=[e4], inc="sS")
        e6 = p.op("scalar", lambda s: s.activation(rstd, rstd, AF.Exp, scale=-0.5),
                  waits=[e5], inc="sS")
        e7 = p.op("vector", lambda v: v.tensor_tensor(ms, mean, rstd, ALU.mult),
                  waits=[e6], inc="sV")
        e8 = p.op("tensor", lambda t: t.matmul(pool[b_r][:, :], ones_row, rstd,
                                               start=True, stop=True),
                  waits=[e6] + (prev_bc if ti == 0 else []), inc="sT")
        e9 = p.op("tensor", lambda t: t.matmul(pool[b_m][:, :], ones_row, ms,
                                               start=True, stop=True),
                  waits=[e7], inc="sT")
        last = None
        for c in range(NCH):
            xc = X[:, c * T + t0: c * T + t0 + 512]
            xnc = Xn[:, c * T + t0: c * T + t0 + 512]
            ea = p.op("vector", lambda v, xc=xc: v.tensor_tensor(
                xnt, xc, pool[b_r][:, :], ALU.mult),
                waits=[e8, e9] + ([last] if last else []), inc="sV")
            last = p.op("vector", lambda v, xnc=xnc: v.tensor_tensor(
                xnc, xnt, pool[b_m][:, :], ALU.subtract),
                waits=[ea], inc="sV")
        done.append(last)
    return done


def emit_moe(p, pool, s, d, scale, store_out=True, ev_ready=(), ev_id_ev=None):
    X, Xn, h, acc = s["X"], s["Xn"], s["h"], s["acc"]
    w1b, w2b = s["w1b"], s["w2b"]

    p.dma("gpsimd", s["wg"].rearrange("p (c e) -> p c e", e=8),
          d["Wg"].rearrange("(c p) e -> p c e", p=128), ch="inG")
    p.dma("sync", s["bgs"], d["bgt"], ch="in")
    p.dma("sync", s["b1s"], d["b1c"], ch="in")
    p.dma("gpsimd", s["b2dev"], d["b2f"], ch="inG")
    ev_in = p.total("in")
    ev_inG = p.total("inG")
    ev_wg = ev_b2 = ev_inG
    ev_bg = ev_b1 = ev_in
    ev_ready = list(ev_ready) + [ev_in, ev_inG]

    ev_ln = emit_ln(p, pool, (0, 1, 2, 3), X, Xn, s["sq2"], s["xnt"], s["ones_col"],
                    s["ones_row"], s["stats"], T_CORE, ev_ready)

    logit, gmask, gtmp, gstat, wAll = s["logit"], s["gmask"], s["gtmp"], s["gstat"], s["wAll"]
    ev_logit = []
    for g in range(16):
        t0 = g * 128
        evm = None
        for c in range(NCH):
            evm = p.op("tensor", lambda t, c=c, t0=t0: t.matmul(
                pool[4][:, 0:8],
                Xn[:, c * T_CORE + t0: c * T_CORE + t0 + 128],
                s["wg"][:, c * 8:(c + 1) * 8],
                start=(c == 0), stop=(c == NCH - 1)),
                waits=[ev_ln[t0 // 512], ev_wg] + ([ev_logit[-1]] if ev_logit else []),
                inc="sT")
        ev_logit.append(p.op("vector", lambda v, g=g: v.tensor_tensor(
            logit[:, g * 8:(g + 1) * 8], pool[4][:, 0:8], s["bgs"], ALU.add),
            waits=[evm, ev_bg], inc="sV"))
    l3 = logit.rearrange("p (g e) -> p g e", e=8)
    g3 = gmask.rearrange("p (g e) -> p g e", e=8)
    t3 = gtmp.rearrange("p (g e) -> p g e", e=8)
    w3 = wAll.rearrange("p (g e) -> p g e", e=8)
    m1, m2 = gstat[:, 0:16], gstat[:, 16:32]
    p1, p2 = gstat[:, 32:48], gstat[:, 48:64]
    r31 = lambda a: a.rearrange("p (g o) -> p g o", o=1)
    bc = lambda a: r31(a).to_broadcast([128, 16, 8])
    e1 = p.op("vector", lambda v: v.tensor_reduce(r31(m1), l3, AX.X, ALU.max),
              waits=[ev_logit[-1]], inc="sV")
    e2 = p.op("vector", lambda v: v.tensor_tensor(g3, l3, bc(m1), ALU.is_equal),
              waits=[e1], inc="sV")
    e3 = p.op("vector", lambda v: v.scalar_tensor_tensor(t3, g3, -1e30, l3, ALU.mult, ALU.add),
              waits=[e2], inc="sV")
    e4 = p.op("vector", lambda v: v.tensor_reduce(r31(m2), t3, AX.X, ALU.max),
              waits=[e3], inc="sV")
    e5 = p.op("vector", lambda v: v.tensor_tensor(p2, m2, m1, ALU.subtract),
              waits=[e4], inc="sV")
    e6 = p.op("scalar", lambda sc: sc.activation(p2, p2, AF.Exp), waits=[e5], inc="sS")
    e7 = p.op("scalar", lambda sc: sc.add(p1, p2, 1.0), waits=[e6], inc="sS")
    e8 = p.op("vector", lambda v: v.reciprocal(p1, p1), waits=[e7], inc="sV")
    e9 = p.op("vector", lambda v: v.tensor_tensor(p2, p2, p1, ALU.mult),
              waits=[e8, e6], inc="sV")
    ea = p.op("vector", lambda v: v.tensor_tensor(g3, g3, bc(p1), ALU.mult),
              waits=[e8], inc="sV")
    eb = p.op("vector", lambda v: v.tensor_tensor(t3, t3, bc(m2), ALU.is_equal),
              waits=[e4, e9], inc="sV")
    ec = p.op("vector", lambda v: v.tensor_tensor(t3, t3, bc(p2), ALU.mult),
              waits=[eb, e9], inc="sV")
    ev_gate = p.op("vector", lambda v: v.tensor_tensor(w3, g3, t3, ALU.add),
                   waits=[ea, ec], inc="sV")
    ev_wT = None
    for g in range(16):
        evt = p.op("tensor", lambda t, g=g: t.matmul(
            pool[4][0:8, 0:128], wAll[:, g * 8:(g + 1) * 8], s["ident"],
            start=True, stop=True, is_transpose=True),
            waits=[ev_gate, ev_logit[-1], ev_wT, ev_in], inc="sT")
        ev_wT = p.op("vector", lambda v, g=g: v.tensor_copy(
            s["wT"][:, g * 128:(g + 1) * 128], pool[4][0:8, 0:128]),
            waits=[evt], inc="sV")

    W1r = d["W1e"]
    W2r = d["W2e"]
    w1_load_idx = 0
    w2_load_idx = 0
    ev_w1use = {}
    ev_w2use = {}
    ev_evac = {}
    ev_tile_done = []
    ev_epi = {0: ev_ln[-1], 1: ev_ln[-1], 2: ev_ln[-1], 3: ev_ln[-1],
              4: ev_wT, 5: None, 6: None, 7: None}
    for ti in range(4):
        t0 = ti * 512
        ev_trans_prev = ev_tile_done[-1] if ev_tile_done else None
        for e in range(8):
            ev_hev = {}
            for fb in range(8):
                par = w1_load_idx % 2
                ev_ld = p.dma("gpsimd", w1b[par].rearrange("p (c f) -> p c f", c=NCH),
                              W1r[e][:, :, fb * 512:(fb + 1) * 512],
                              waits=[ev_w1use.get(w1_load_idx - 2)], ch=f"w1{par}")
                w1_load_idx += 1
                last_mm = None
                for fc in range(4):
                    fci = fb * 4 + fc
                    bank = 4 + (fci % 2)
                    mm = None
                    for dc in range(NCH):
                        mm = p.op("tensor", lambda t, fc=fc, dc=dc, par=par, bank=bank, t0=t0: t.matmul(
                            pool[bank][:, :],
                            w1b[par][:, dc * 512 + fc * 128: dc * 512 + fc * 128 + 128],
                            Xn[:, dc * T_CORE + t0: dc * T_CORE + t0 + 512],
                            start=(dc == 0), stop=(dc == NCH - 1)),
                            waits=([ev_ld, ev_ln[ti], ev_evac.get((ti, e, fci - 2)),
                                    ev_epi.get(4 + (fci % 2))] if dc == 0 else []),
                            inc="sT")
                    ev_evac[(ti, e, fci)] = p.op(
                        "scalar", lambda sc, fci=fci, bank=bank, e=e: sc.activation(
                            h[:, fci * 512:(fci + 1) * 512], pool[bank][:, :], AF.Gelu,
                            bias=s["b1s"][:, e * 32 + fci: e * 32 + fci + 1]),
                        waits=[mm, ev_b1], inc="sS")
                    last_mm = mm
                ev_w1use[w1_load_idx - 1] = last_mm
            ev_h_done = ev_evac[(ti, e, 31)]
            first_of_bank = {k: True for k in range(8)}
            ev_mm_bank = {}
            for fb in range(8):
                par = w2_load_idx % 2
                ev_ld2 = p.dma("gpsimd", w2b[par].rearrange("p (c dd) -> p c dd", c=4),
                               W2r[e][:, fb * 4:(fb + 1) * 4, :],
                               waits=[ev_w2use.get(w2_load_idx - 2)], ch=f"w2{par}")
                w2_load_idx += 1
                last2 = None
                for g in range(4):
                    for dh in range(2):
                        bank = g * 2 + dh
                        if first_of_bank[bank] and e == 0:
                            p.op("tensor", lambda t, g=g, dh=dh, bank=bank, ti=ti: t.matmul(
                                pool[bank][:, :],
                                s["wT"][:, (ti * 4 + g) * 128:(ti * 4 + g) * 128 + 128],
                                s["b2dev"][:, dh * 512: dh * 512 + 512],
                                start=True, stop=False),
                                waits=[ev_b2, ev_wT, ev_epi.get(bank), ev_h_done,
                                       ev_trans_prev], inc="sT")
                            first_of_bank[bank] = False
                        for fc in range(4):
                            fci = fb * 4 + fc
                            first_mm = first_of_bank[bank]
                            first_of_bank[bank] = False
                            ww = [ev_ld2] if (g == 0 and dh == 0 and fc == 0) else []
                            if first_mm:
                                ww += [ev_epi.get(bank), ev_h_done, ev_trans_prev]
                            last2 = p.op("tensor", lambda t, fci=fci, g=g, dh=dh, par=par, bank=bank, first_mm=first_mm, e=e, fb=fb, fc=fc: t.matmul(
                                pool[bank][:, :],
                                h[:, fci * 512 + g * 128: fci * 512 + g * 128 + 128],
                                w2b[par][:, fc * 1024 + dh * 512: fc * 1024 + dh * 512 + 512],
                                start=(first_mm and e > 0), stop=(fb == 7 and fc == 3)),
                                waits=ww, inc="sT")
                            ev_mm_bank[bank] = last2
                ev_w2use[w2_load_idx - 1] = last2
            for g in range(4):
                for dh in range(2):
                    bank = g * 2 + dh
                    wcol = wAll[:, (ti * 4 + g) * 8 + e:(ti * 4 + g) * 8 + e + 1]
                    dst = acc[:, g * 1024 + dh * 512: g * 1024 + dh * 512 + 512]
                    if e == 0:
                        ev_epi[bank] = p.op("vector", lambda v, bank=bank, wcol=wcol, dst=dst: v.tensor_scalar(
                            dst, pool[bank][:, :], wcol, None, ALU.mult),
                            waits=[ev_mm_bank[bank], ev_gate, ev_trans_prev], inc="sV")
                    else:
                        ev_epi[bank] = p.op("vector", lambda v, bank=bank, wcol=wcol, dst=dst: v.scalar_tensor_tensor(
                            dst, pool[bank][:, :], wcol, dst, ALU.mult, ALU.add),
                            waits=[ev_mm_bank[bank], ev_gate, ev_epi[bank]], inc="sV")
        ev_ep_fin = [ev_epi[k] for k in range(8)]
        ev_t_tile = []
        ev_copy_prev = None
        evt_last = None
        ev_bankread = {0: None, 1: None}
        for c in range(NCH):
            ev_tr = {}
            for g in range(4):
                bank = g % 2
                evt = p.op("tensor", lambda t, c=c, g=g, bank=bank: t.matmul(
                    pool[bank][:, 0:128],
                    acc[:, g * 1024 + c * 128: g * 1024 + c * 128 + 128],
                    s["ident"], start=True, stop=True, is_transpose=True),
                    waits=ev_ep_fin + [ev_bankread[bank]], inc="sT")
                evt_last = evt
                ev_tr[g] = p.op("vector", lambda v, c=c, g=g, t0=t0, bank=bank: v.scalar_tensor_tensor(
                    s["xout"][:, g * 128:(g + 1) * 128], pool[bank][:, 0:128], scale,
                    X[:, c * T_CORE + t0 + g * 128: c * T_CORE + t0 + g * 128 + 128],
                    ALU.mult, ALU.add),
                    waits=[evt, ev_copy_prev], inc="sV")
                ev_bankread[bank] = ev_tr[g]
                ev_epi[bank] = ev_tr[g]
            ev_copy_prev = p.op("vector", lambda v, c=c, t0=t0: v.tensor_copy(
                X[:, c * T_CORE + t0: c * T_CORE + t0 + 512], s["xout"]),
                waits=[ev_tr[0], ev_tr[1], ev_tr[2], ev_tr[3]], inc="sV")
            evw = ev_copy_prev
            if store_out:
                evw = p.dma("sync", d["yT"][c * 128:(c + 1) * 128, t0:t0 + 512],
                            X[:, c * T_CORE + t0: c * T_CORE + t0 + 512], waits=[evw],
                            ch="out")
            ev_t_tile.append(evw)
        ev_tile_done.append(evt_last)
    return ev_tile_done


def moe_sbufs(nc, ctx):
    mk = lambda name, shape, dt=F32: ctx.enter_context(nc.sbuf_tensor(name, shape, dt))[:, :]
    s = {}
    s["X"] = mk("X", [128, NCH * T_CORE])
    s["Xn"] = mk("Xn", [128, NCH * T_CORE], BF16)
    s["h"] = mk("h", [128, 32 * 512], BF16)
    s["acc"] = mk("acc", [128, 4 * 1024])
    s["w1b"] = [mk(f"w1b{i}", [128, 4096], BF16) for i in range(2)]
    s["w2b"] = [mk(f"w2b{i}", [128, 4096], BF16) for i in range(2)]
    s["sq2"] = mk("sq2", [128, 1024])
    s["xnt"] = mk("xnt", [128, 512])
    s["stats"] = mk("stats", [1, 2048])
    s["ones_col"] = mk("ones_col", [128, 1])
    s["ones_row"] = mk("ones_row", [1, 128])
    s["ones_row_bf"] = mk("ones_row_bf", [1, 128], BF16)
    s["wg"] = mk("wg", [128, 64], BF16)
    s["bgs"] = mk("bgs", [128, 8])
    s["logit"] = mk("logit", [128, 128])
    s["gmask"] = mk("gmask", [128, 128])
    s["gtmp"] = mk("gtmp", [128, 128])
    s["gstat"] = mk("gstat", [128, 64])
    s["wAll"] = mk("wAll", [128, 128])
    s["b1s"] = mk("b1s", [128, 256])
    s["b2dev"] = mk("b2dev", [8, 1024], BF16)
    s["wT"] = mk("wT", [8, 2048], BF16)
    s["xout"] = mk("xout", [128, 512])
    s["ident"] = mk("ident", [128, 128])
    return s


def build_moe_program(scale=0.5):
    nc = bass.Bass("TRN2", target_bir_lowering=False)
    d = {}
    d["xT"] = nc.dram_tensor("xT", [D_MODEL, T_CORE], F32, kind="ExternalInput").ap()
    d["Wg"] = nc.dram_tensor("Wg", [D_MODEL, 8], F32, kind="ExternalInput").ap()
    d["bgt"] = nc.dram_tensor("bgt", [128, 8], F32, kind="ExternalInput").ap()
    W1 = nc.dram_tensor("W1", [8, D_MODEL, 4096], F32, kind="ExternalInput").ap()
    d["b1c"] = nc.dram_tensor("b1c", [128, 256], F32, kind="ExternalInput").ap()
    W2 = nc.dram_tensor("W2", [8, 4096, D_MODEL], F32, kind="ExternalInput").ap()
    d["b2f"] = nc.dram_tensor("b2f", [8, 1024], F32, kind="ExternalInput").ap()
    d["identD"] = nc.dram_tensor("identD", [128, 128], F32, kind="ExternalInput").ap()
    d["yT"] = nc.dram_tensor("yT", [D_MODEL, T_CORE], F32, kind="ExternalOutput").ap()
    d["W1e"] = [W1[e].rearrange("(c p) f -> p c f", p=128) for e in range(8)]
    d["W2e"] = [W2[e].rearrange("(c p) dd -> p c dd", p=128) for e in range(8)]

    with ExitStack() as ctx:
        p = Prog(nc, ctx)
        pool = [ctx.enter_context(nc.psum_tensor(f"pp{i}", [128, 512], F32)) for i in range(8)]
        s = moe_sbufs(nc, ctx)
        p.dma("sync", s["ident"], d["identD"], ch="in")
        for c in range(NCH):
            p.dma("sync", s["X"][:, c * T_CORE:(c + 1) * T_CORE],
                  d["xT"][c * 128:(c + 1) * 128, :], ch="in")
        p.op("vector", lambda v: v.memset(s["ones_col"], 1.0), inc="sV")
        p.op("vector", lambda v: v.memset(s["ones_row"], 1.0), inc="sV")
        p.op("vector", lambda v: v.memset(s["ones_row_bf"], 1.0), inc="sV")
        cp = ConstPool(nc, ctx, p)
        ev_c = cp.get(LN_EPS)
        emit_moe(p, pool, s, d, scale, store_out=True,
                 ev_ready=[ev_c], ev_id_ev=None)
        block = ctx.enter_context(nc.Block())
        p.finish(block)
    return nc


def shard_x(x):
    out = []
    for b in range(2):
        for q in range(4):
            out.append(np.ascontiguousarray(x[b, q * 2048:(q + 1) * 2048, :].T))
    return out


def unshard_x(shards):
    x = np.empty((2, 8192, 1024), np.float32)
    for b in range(2):
        for q in range(4):
            x[b, q * 2048:(q + 1) * 2048, :] = shards[b * 4 + q].T
    return x


def moe_in_map(xs, W):
    return dict(
        xT=xs,
        Wg=np.ascontiguousarray(W["Wg"]),
        bgt=np.ascontiguousarray(np.tile(W["bg"][None, :], (128, 1))),
        W1=np.ascontiguousarray(W["W1"]),
        b1c=np.ascontiguousarray(W["b1"].reshape(8, 32, 128).transpose(2, 0, 1).reshape(128, 256)),
        W2=np.ascontiguousarray(W["W2"]),
        b2f=np.ascontiguousarray(W["b2"]),
        identD=np.eye(128, dtype=np.float32),
    )


def tern_sbufs(nc, ctx):
    mk = lambda name, shape, dt=F32: ctx.enter_context(nc.sbuf_tensor(name, shape, dt))[:, :]
    s = {}
    arena = mk("arena", [128, 65536], BF16)
    af = arena.bitcast(F32)
    s["arena"] = arena
    s["X"] = af[:, 0:NCH * T_CORE]
    s["H"] = arena[:, 0:32768]
    s["H2"] = arena[:, 32768:65536]
    s["sq2"] = af[:, 16384:17408]
    s["xnt"] = af[:, 17408:17920]
    s["stats"] = af[0:1, 17920:19968]
    s["Xn"] = mk("Xn_t", [128, NCH * T_CORE], BF16)
    s["wa"] = [mk(f"twa{i}", [128, 1024], BF16) for i in range(2)]
    s["wb"] = [mk(f"twb{i}", [128, 1024], BF16) for i in range(2)]
    s["wh"] = [mk(f"twh{i}", [128, 4096], BF16) for i in range(2)]
    s["ones_col"] = mk("ones_col_t", [128, 1])
    s["ones_row"] = mk("ones_row_t", [1, 128])
    s["scrA"] = mk("scrA", [128, 512])
    s["scrB"] = mk("scrB", [128, 512])
    s["scrC"] = mk("scrC", [128, 512])
    s["scrD"] = mk("scrD", [128, 512])
    s["scal"] = mk("scal", [128, 32 * 4 + 16])
    s["xres"] = mk("xres_t", [128, 1024])
    return s


def emit_ternary(p, pool, s, d, scale, store_out=True, ev_ready=()):
    X, Xn, H, H2 = s["X"], s["Xn"], s["H"], s["H2"]
    scal = s["scal"]
    s1c, s2c = scal[:, 0:32], scal[:, 32:64]
    hs1c, hs2c = scal[:, 64:96], scal[:, 96:128]
    s3c, s3h = scal[:, 128:136], scal[:, 136:144]
    scr = [s["scrA"], s["scrB"], s["scrC"], s["scrD"]]

    ev_sc = p.dma("sync", scal[:, 0:136], d["scales"], ch="sc")
    ev_s3h = p.op("vector", lambda v: v.tensor_scalar(s3h, s3c, scale, None, ALU.mult),
                  waits=[ev_sc], inc="sV")

    ev_ln = emit_ln(p, pool, (0, 1, 2, 3), X, Xn, s["sq2"], s["xnt"], s["ones_col"],
                    s["ones_row"], s["stats"], T_CORE, list(ev_ready))
    ev_lnall = ev_ln[-1]

    wa, wb, wh = s["wa"], s["wb"], s["wh"]
    ld_a = 0
    ld_h = 0
    ev_wause = {}
    ev_wbuse = {}
    ev_whuse = {}
    ev_bank = {k: ev_lnall for k in range(8)}
    ev_scr = {0: None, 1: None, 2: None, 3: None}
    ev_out_done = []
    ev_ostore = {0: None, 1: None}
    for half in range(2):
        th0 = half * 1024
        evA_last = None
        for fc in range(32):
            par = ld_a % 2
            eva = p.dma("gpsimd", wa[par].rearrange("p (c f) -> p c f", c=NCH),
                        d["w1T"][:, :, fc * 128:(fc + 1) * 128],
                        waits=[ev_wause.get(ld_a - 2)], ch=f"ta{par}")
            evb = p.dma("gpsimd", wb[par].rearrange("p (c f) -> p c f", c=NCH),
                        d["w2T"][:, :, fc * 128:(fc + 1) * 128],
                        waits=[ev_wbuse.get(ld_a - 2)], ch=f"tb{par}")
            ld_a += 1
            for sub in range(2):
                t0 = th0 + sub * 512
                b1, b2 = 4 + sub, 6 + sub
                m1 = m2 = None
                for dc in range(NCH):
                    m1 = p.op("tensor", lambda t, dc=dc, par=par, t0=t0, b1=b1: t.matmul(
                        pool[b1][:, :], wa[par][:, dc * 128:(dc + 1) * 128],
                        Xn[:, dc * T_CORE + t0: dc * T_CORE + t0 + 512],
                        start=(dc == 0), stop=(dc == NCH - 1)),
                        waits=([eva, ev_lnall, ev_bank[b1]] if dc == 0 else []), inc="sT")
                for dc in range(NCH):
                    m2 = p.op("tensor", lambda t, dc=dc, par=par, t0=t0, b2=b2: t.matmul(
                        pool[b2][:, :], wb[par][:, dc * 128:(dc + 1) * 128],
                        Xn[:, dc * T_CORE + t0: dc * T_CORE + t0 + 512],
                        start=(dc == 0), stop=(dc == NCH - 1)),
                        waits=([evb, ev_bank[b2]] if dc == 0 else []), inc="sT")
                sA, sB = scr[sub], scr[2 + sub]
                ea1 = p.op("vector", lambda v, fc=fc, b1=b1, sA=sA: v.tensor_scalar(
                    sA, pool[b1][:, :], s1c[:, fc:fc + 1], None, ALU.mult),
                    waits=[m1, ev_sc, ev_bank[b1], ev_scr[sub]], inc="sV")
                es = p.op("scalar", lambda sc, sA=sA: sc.activation(sA, sA, AF.Silu),
                          waits=[ea1], inc="sS")
                ea2 = p.op("vector", lambda v, fc=fc, b2=b2, sB=sB: v.tensor_scalar(
                    sB, pool[b2][:, :], s2c[:, fc:fc + 1], None, ALU.mult),
                    waits=[m2, ev_sc, ev_scr[2 + sub]], inc="sV")
                dst = H[:, fc * 1024 + sub * 512: fc * 1024 + sub * 512 + 512]
                eh = p.op("vector", lambda v, dst=dst, sA=sA, sB=sB: v.tensor_tensor(
                    dst, sA, sB, ALU.mult),
                    waits=[ea2, es], inc="sV")
                ev_bank[b1] = ea1
                ev_bank[b2] = ea2
                ev_scr[sub] = eh
                ev_scr[2 + sub] = eh
                evA_last = eh
            ev_wause[ld_a - 1] = m1
            ev_wbuse[ld_a - 1] = m2
        ev_src = evA_last
        for stage in range(2):
            src_t = H if stage == 0 else H2
            dst_t = H2 if stage == 0 else H
            scol = hs1c if stage == 0 else hs2c
            wd = d["h1T"] if stage == 0 else d["h2T"]
            evB_last = None
            for fo in range(32):
                par = ld_h % 2
                evw = p.dma("gpsimd", wh[par].rearrange("p (c f) -> p c f", c=32),
                            wd[:, :, fo * 128:(fo + 1) * 128],
                            waits=[ev_whuse.get(ld_h - 2)], ch=f"th{par}")
                ld_h += 1
                mm = None
                for sub in range(2):
                    bk = (fo % 2) * 2 + sub
                    for ci in range(32):
                        mm = p.op("tensor", lambda t, ci=ci, par=par, sub=sub, bk=bk, src_t=src_t: t.matmul(
                            pool[bk][:, :], wh[par][:, ci * 128:(ci + 1) * 128],
                            src_t[:, ci * 1024 + sub * 512: ci * 1024 + sub * 512 + 512],
                            start=(ci == 0), stop=(ci == 31)),
                            waits=([evw, ev_src, ev_bank[bk]] if ci == 0 else []), inc="sT")
                    et = p.op("vector", lambda v, bk=bk, fo=fo, scol=scol: v.tensor_scalar(
                        scr[bk], pool[bk][:, :], scol[:, fo:fo + 1], None, ALU.mult),
                        waits=[mm, ev_sc, ev_bank[bk], ev_scr[bk]], inc="sV")
                    dsl = dst_t[:, fo * 1024 + sub * 512: fo * 1024 + sub * 512 + 512]
                    eg = p.op("scalar", lambda sc, dsl=dsl, bk=bk: sc.activation(
                        dsl, scr[bk], AF.Gelu),
                        waits=[et], inc="sS")
                    ev_bank[bk] = eg
                    ev_scr[bk] = eg
                    evB_last = eg
                ev_whuse[ld_h - 1] = mm
            ev_src = evB_last
        for dc in range(NCH):
            par = ld_h % 2
            evw = p.dma("gpsimd", wh[par].rearrange("p (c f) -> p c f", c=32),
                        d["w3T"][:, :, dc * 128:(dc + 1) * 128],
                        waits=[ev_whuse.get(ld_h - 2)], ch=f"th{par}")
            ld_h += 1
            mm = None
            for sub in range(2):
                t0 = th0 + sub * 512
                bk = 4 + (dc % 2) * 2 + sub
                opar = (dc * 2 + sub) % 2
                for ci in range(32):
                    mm = p.op("tensor", lambda t, ci=ci, par=par, bk=bk: t.matmul(
                        pool[bk][:, :], wh[par][:, ci * 128:(ci + 1) * 128],
                        H[:, ci * 1024 + sub * 512: ci * 1024 + sub * 512 + 512],
                        start=(ci == 0), stop=(ci == 31)),
                        waits=([evw, ev_src, ev_bank[bk]] if ci == 0 else []), inc="sT")
                xrb = s["xres"][:, opar * 512:(opar + 1) * 512]
                evx = p.dma("sync", xrb, d["xT"][dc * 128:(dc + 1) * 128, t0:t0 + 512],
                            waits=[ev_ostore[opar]], ch=f"xr{opar}")
                ocb = scr[opar]
                eo = p.op("vector", lambda v, dc=dc, bk=bk, ocb=ocb, xrb=xrb: v.scalar_tensor_tensor(
                    ocb, pool[bk][:, :], s3h[:, dc:dc + 1], xrb, ALU.mult, ALU.add),
                    waits=[mm, evx, ev_s3h, ev_ostore[opar], ev_bank[bk], ev_scr[opar]], inc="sV")
                ev_bank[bk] = eo
                evs = p.dma("sync", d["yT"][dc * 128:(dc + 1) * 128, t0:t0 + 512],
                            ocb, waits=[eo], ch=f"o{opar}")
                ev_ostore[opar] = evs
                ev_scr[opar] = evs
                ev_out_done.append(evs)
            ev_whuse[ld_h - 1] = mm
    return ev_out_done


def build_ternary_program(scale=0.5):
    nc = bass.Bass("TRN2", target_bir_lowering=False)
    d = {}
    d["xT"] = nc.dram_tensor("xT", [D_MODEL, T_CORE], F32, kind="ExternalInput").ap()
    w1 = nc.dram_tensor("w1T", [D_MODEL, 4096], F32, kind="ExternalInput").ap()
    w2 = nc.dram_tensor("w2T", [D_MODEL, 4096], F32, kind="ExternalInput").ap()
    h1 = nc.dram_tensor("h1T", [4096, 4096], F32, kind="ExternalInput").ap()
    h2 = nc.dram_tensor("h2T", [4096, 4096], F32, kind="ExternalInput").ap()
    w3 = nc.dram_tensor("w3T", [4096, D_MODEL], F32, kind="ExternalInput").ap()
    d["scales"] = nc.dram_tensor("scales", [128, 136], F32, kind="ExternalInput").ap()
    d["yT"] = nc.dram_tensor("yT", [D_MODEL, T_CORE], F32, kind="ExternalOutput").ap()
    d["w1T"] = w1.rearrange("(c p) f -> p c f", p=128)
    d["w2T"] = w2.rearrange("(c p) f -> p c f", p=128)
    d["h1T"] = h1.rearrange("(c p) f -> p c f", p=128)
    d["h2T"] = h2.rearrange("(c p) f -> p c f", p=128)
    d["w3T"] = w3.rearrange("(c p) f -> p c f", p=128)

    with ExitStack() as ctx:
        p = Prog(nc, ctx)
        pool = [ctx.enter_context(nc.psum_tensor(f"pp{i}", [128, 512], F32)) for i in range(8)]
        s = tern_sbufs(nc, ctx)
        for c in range(NCH):
            p.dma("sync", s["X"][:, c * T_CORE:(c + 1) * T_CORE],
                  d["xT"][c * 128:(c + 1) * 128, :], ch="in")
        p.op("vector", lambda v: v.memset(s["ones_col"], 1.0), inc="sV")
        p.op("vector", lambda v: v.memset(s["ones_row"], 1.0), inc="sV")
        cp = ConstPool(nc, ctx, p)
        ev_c = cp.get(LN_EPS)
        ev_in = p.total("in")
        emit_ternary(p, pool, s, d, scale, store_out=True, ev_ready=[ev_c, ev_in])
        block = ctx.enter_context(nc.Block())
        p.finish(block)
    return nc


def tern_in_map(xs, W):
    sc = np.zeros((128, 136), np.float32)
    sc[:, 0:32] = W["w1_s"].reshape(32, 128).T
    sc[:, 32:64] = W["w2_s"].reshape(32, 128).T
    sc[:, 64:96] = W["h1_s"].reshape(32, 128).T
    sc[:, 96:128] = W["h2_s"].reshape(32, 128).T
    sc[:, 128:136] = W["w3_s"].reshape(8, 128).T
    return dict(
        xT=xs,
        w1T=np.ascontiguousarray(W["w1_w"].T),
        w2T=np.ascontiguousarray(W["w2_w"].T),
        h1T=np.ascontiguousarray(W["h1_w"].T),
        h2T=np.ascontiguousarray(W["h2_w"].T),
        w3T=np.ascontiguousarray(W["w3_w"].T),
        scales=sc,
    )


def attn_sbufs(nc, ctx):
    mk = lambda name, shape, dt=F32: ctx.enter_context(nc.sbuf_tensor(name, shape, dt))[:, :]
    s = {}
    arena = mk("arena_a", [128, 32768], BF16)
    af = arena.bitcast(F32)
    s["X"] = af[:, 0:NCH * T_CORE]
    s["qf"] = arena[:, 0:16384]
    s["ktm"] = arena[:, 16384:32768]
    vt = mk("vtm", [128, 16640], BF16)
    s["vtm"] = vt
    s["oall"] = vt[:, 0:16384]
    vf = vt.bitcast(F32)
    s["sq2"] = vf[:, 0:1024]
    s["xnt"] = vf[:, 1024:1536]
    s["stats"] = vf[0:1, 1536:3584]
    s["Xn"] = mk("Xn_a", [128, NCH * T_CORE], BF16)
    s["wq"] = mk("awq", [128, 8192], BF16)
    wkv = mk("awkv", [128, 16384], BF16)
    s["wkv"] = wkv
    s["wproj"] = wkv[:, 0:8192]
    s["ones_col"] = mk("ones_col_a", [128, 1])
    s["ones_row"] = mk("ones_row_a", [1, 128])
    s["ones64_bf"] = mk("ones64_bf", [1, 64], BF16)
    s["kvloc"] = mk("kvloc", [128, 520])
    s["kvred"] = mk("kvred", [128, 520])
    s["kvbf"] = mk("kvbf", [128, 520], BF16)
    s["normr32"] = mk("normr32", [1, 1024])
    s["normri"] = mk("normri", [1, 1024])
    s["normrow2"] = mk("normrow2", [1, 1024], BF16)
    s["normb"] = mk("normb_a", [128, 512])
    s["scrA"] = mk("scrA_a", [128, 512])
    s["scrB"] = mk("scrB_a", [128, 512])
    s["xres"] = mk("xres_a", [128, 1024])
    s["ocb"] = mk("ocb_a", [128, 1024])
    s["bproj"] = mk("bproj_a", [128, 8])
    return s


def emit_attn(p, nc, pool, s, d, store_out=True, ev_ready=()):
    X, Xn, qf, ktm, vtm = s["X"], s["Xn"], s["qf"], s["ktm"], s["vtm"]
    EPS = 1e-6
    scr = [s["scrA"], s["scrB"]]

    ev_bp = p.dma("sync", s["bproj"], d["bprojT"], ch="sc")
    ev_wq = p.dma("gpsimd", s["wq"].rearrange("p (c f) -> p c f", c=NCH),
                  d["WqT"].rearrange("(c p) f -> p c f", p=128), ch="awq")
    ev_wkv = p.dma("gpsimd", s["wkv"].rearrange("p (c f) -> p c f", c=NCH),
                   d["WkvT"].rearrange("(c p) f -> p c f", p=128), ch="awk")

    ev_ln = emit_ln(p, pool, (0, 1, 2, 3), X, Xn, s["sq2"], s["xnt"], s["ones_col"],
                    s["ones_row"], s["stats"], T_CORE, list(ev_ready))
    ev_lnall = ev_ln[-1]
    ev_vset = p.op("vector", lambda v: v.memset(vtm[:, :], 1.0), waits=[ev_lnall], inc="sV")

    ev_bank = {k: ev_lnall for k in range(8)}
    ev_scr = {0: None, 1: None}

    def phi_evac(mm, bk, sc_i, dst):
        e1 = p.op("vector", lambda v, bk=bk, sc_i=sc_i: v.tensor_scalar(
            scr[sc_i], pool[bk][:, :], 0.0, None, ALU.min),
            waits=[mm, ev_scr[sc_i]], inc="sV")
        e2 = p.op("scalar", lambda sc, sc_i=sc_i: sc.activation(
            scr[sc_i], scr[sc_i], AF.Exp), waits=[e1], inc="sS")
        e3 = p.op("vector", lambda v, bk=bk, sc_i=sc_i, dst=dst: v.scalar_tensor_tensor(
            dst, pool[bk][:, :], 1.0, scr[sc_i], ALU.add, ALU.max),
            waits=[e2], inc="sV")
        ev_bank[bk] = e3
        ev_scr[sc_i] = e3
        return e3

    ev_qdone = None
    for qc in range(NCH):
        for ti in range(4):
            t0 = ti * 512
            bk = 4 + (ti % 2)
            mm = None
            for dc in range(NCH):
                mm = p.op("tensor", lambda t, dc=dc, t0=t0, bk=bk, qc=qc: t.matmul(
                    pool[bk][:, :],
                    s["wq"][:, dc * 1024 + qc * 128: dc * 1024 + qc * 128 + 128],
                    Xn[:, dc * T_CORE + t0: dc * T_CORE + t0 + 512],
                    start=(dc == 0), stop=(dc == NCH - 1)),
                    waits=([ev_wq, ev_ln[ti], ev_bank[bk]] if dc == 0 else []), inc="sT")
            ev_qdone = phi_evac(mm, bk, ti % 2,
                                qf[:, qc * T_CORE + t0: qc * T_CORE + t0 + 512])

    ev_kv_ev = None
    ev_inproj_mm = None
    for g in range(16):
        t0 = g * 128
        for col in range(4):
            bk = col
            mm = None
            for dc in range(NCH):
                mm = p.op("tensor", lambda t, dc=dc, t0=t0, col=col, bk=bk: t.matmul(
                    pool[bk][:, :],
                    Xn[:, dc * T_CORE + t0: dc * T_CORE + t0 + 128],
                    s["wkv"][:, dc * 2048 + col * 512: dc * 2048 + col * 512 + 512],
                    start=(dc == 0), stop=(dc == NCH - 1)),
                    waits=([ev_wkv, ev_ln[g // 4], ev_bank[bk]] if dc == 0 else []), inc="sT")
            ev_inproj_mm = mm
            if col < 2:
                ev_kv_ev = phi_evac(mm, bk, col,
                                    ktm[:, g * 1024 + col * 512: g * 1024 + col * 512 + 512])
            else:
                dst = vtm.rearrange("p (g h e) -> p g h e", g=16, h=16)[
                    :, g, (col - 2) * 8:(col - 2) * 8 + 8, 0:64]
                ev_kv_ev = p.op("vector", lambda v, bk=bk, dst=dst: v.tensor_copy(
                    dst, pool[bk][:, :].rearrange("p (h e) -> p h e", h=8)),
                    waits=[mm, ev_vset], inc="sV")
                ev_bank[bk] = ev_kv_ev

    ev_wp = p.dma("gpsimd", s["wproj"].rearrange("p (c f) -> p c f", c=NCH),
                  d["WprojT"].rearrange("(c p) f -> p c f", p=128),
                  waits=[ev_inproj_mm], ch="awp")

    ev_kvmm = {}
    ev_kvloc = None
    for r in range(2):
        for g in range(16):
            for b in range(8):
                h = 2 * b + r
                r0 = r * 64
                ev_kvmm[h] = p.op("tensor", lambda t, g=g, h=h, b=b, r0=r0: t.matmul(
                    pool[b][r0:r0 + 64, 0:65],
                    ktm[:, g * 1024 + h * 64: g * 1024 + h * 64 + 64],
                    vtm[:, g * 1040 + h * 65: g * 1040 + h * 65 + 65],
                    start=(g == 0), stop=(g == 15)),
                    waits=([ev_kv_ev, ev_bank[b]] if g == 0 else []), inc="sT")
        for b in range(8):
            h = 2 * b + r
            r0 = r * 64
            ev_kvloc = p.op("vector", lambda v, b=b, r0=r0: v.tensor_copy(
                s["kvloc"][r0:r0 + 64, b * 65:(b + 1) * 65], pool[b][r0:r0 + 64, 0:65]),
                waits=[ev_kvmm[h]], inc="sV")
            ev_bank[b] = ev_kvloc

    ev_c1 = p.dma("gpsimd", d["cc_in"], s["kvloc"], waits=[ev_kvloc], ch="cc1")
    ev_cc = p.op("gpsimd", lambda gp: gp.collective_compute(
        "AllReduce", ALU.add, replica_groups=[[0, 1, 2, 3], [4, 5, 6, 7]],
        ins=[d["cc_in"].opt()], outs=[d["cc_out"].opt()]),
        waits=[ev_c1], inc="sG")
    ev_c2 = p.dma("gpsimd", s["kvred"], d["cc_out"], waits=[ev_cc], ch="cc2")
    ev_kvbf = p.op("vector", lambda v: v.tensor_copy(s["kvbf"], s["kvred"]),
                   waits=[ev_c2], inc="sV")

    ev_ost = {0: None, 1: None}
    obk = 0
    out_done = []
    for ti in range(4):
        t0 = ti * 512
        for hc in range(NCH):
            bo = obk % 2
            bn = 2 + (obk % 2)
            bb = 4 + (obk % 2)
            obk += 1
            mos = []
            for par in range(2):
                r0 = par * 64
                mo = p.op("tensor", lambda t, hc=hc, r0=r0, t0=t0, bo=bo: t.matmul(
                    pool[bo][r0:r0 + 64, :],
                    s["kvbf"][r0:r0 + 64, hc * 65: hc * 65 + 64],
                    qf[r0:r0 + 64, hc * T_CORE + t0: hc * T_CORE + t0 + 512],
                    start=True, stop=True),
                    waits=[ev_kvbf, ev_bank[bo]], inc="sT")
                mn = p.op("tensor", lambda t, hc=hc, r0=r0, t0=t0, bn=bn: t.matmul(
                    pool[bn][r0:r0 + 1, :],
                    s["kvbf"][r0:r0 + 64, hc * 65 + 64: hc * 65 + 65],
                    qf[r0:r0 + 64, hc * T_CORE + t0: hc * T_CORE + t0 + 512],
                    start=True, stop=True),
                    waits=[ev_kvbf, ev_bank[bn]], inc="sT")
                e1 = p.op("vector", lambda v, r0=r0, bn=bn, par=par: v.tensor_scalar(
                    s["normr32"][0:1, par * 512:(par + 1) * 512],
                    pool[bn][r0:r0 + 1, :], EPS, None, ALU.add),
                    waits=[mn], inc="sV")
                mos.append((mo, e1))
            ev_bank[bn] = mos[1][1]
            e2a = p.op("vector", lambda v: v.reciprocal(s["normri"], s["normr32"]),
                       waits=[mos[0][1], mos[1][1]], inc="sV")
            e2 = p.op("vector", lambda v: v.tensor_copy(s["normrow2"], s["normri"]),
                      waits=[e2a], inc="sV")
            eb0 = p.op("tensor", lambda t, bb=bb: t.matmul(
                pool[bb][0:64, :], s["ones64_bf"], s["normrow2"][0:1, 0:512],
                start=True, stop=True), waits=[e2, ev_bank[bb]], inc="sT")
            eb1 = p.op("tensor", lambda t, bb=bb: t.matmul(
                pool[bb][64:128, :], s["ones64_bf"], s["normrow2"][0:1, 512:1024],
                start=True, stop=True), waits=[e2], inc="sT")
            e4 = p.op("vector", lambda v, bb=bb: v.tensor_copy(s["normb"], pool[bb][:, :]),
                      waits=[eb1, eb0], inc="sV")
            ev_bank[bb] = e4
            dst = s["oall"][:, hc * 2048 + t0: hc * 2048 + t0 + 512]
            e5 = p.op("vector", lambda v, bo=bo, dst=dst: v.tensor_tensor(
                dst, pool[bo][:, :], s["normb"], ALU.mult),
                waits=[e4, mos[0][0], mos[1][0]], inc="sV")
            ev_bank[bo] = e5
            ev_oall = e5
    for ti in range(4):
        t0 = ti * 512
        for oc in range(NCH):
            bk = 6 + (oc % 2)
            opar = oc % 2
            mm = None
            for dc in range(NCH):
                mm = p.op("tensor", lambda t, dc=dc, oc=oc, t0=t0, bk=bk: t.matmul(
                    pool[bk][:, :],
                    s["wproj"][:, dc * 1024 + oc * 128: dc * 1024 + oc * 128 + 128],
                    s["oall"][:, dc * 2048 + t0: dc * 2048 + t0 + 512],
                    start=(dc == 0), stop=(dc == NCH - 1)),
                    waits=([ev_wp, ev_oall, ev_bank[bk]] if dc == 0 else []), inc="sT")
            xrb = s["xres"][:, opar * 512:(opar + 1) * 512]
            evx = p.dma("sync", xrb, d["xT"][oc * 128:(oc + 1) * 128, t0:t0 + 512],
                        waits=[ev_ost[opar]], ch=f"xr{opar}")
            ocb = s["ocb"][:, opar * 512:(opar + 1) * 512]
            eo = p.op("vector", lambda v, oc=oc, bk=bk, ocb=ocb, xrb=xrb: v.scalar_tensor_tensor(
                ocb, pool[bk][:, :], s["bproj"][:, oc:oc + 1], xrb, ALU.add, ALU.add),
                waits=[mm, evx, ev_bp, ev_ost[opar]], inc="sV")
            ev_bank[bk] = eo
            evs = p.dma("sync", d["yT"][oc * 128:(oc + 1) * 128, t0:t0 + 512],
                        ocb, waits=[eo], ch=f"o{opar}")
            ev_ost[opar] = evs
            out_done.append(evs)
    return out_done


def build_attn_program():
    nc = bass.Bass("TRN2", target_bir_lowering=False)
    d = {}
    d["xT"] = nc.dram_tensor("xT", [D_MODEL, T_CORE], F32, kind="ExternalInput").ap()
    d["WqT"] = nc.dram_tensor("WqT", [D_MODEL, 1024], F32, kind="ExternalInput").ap()
    d["WkvT"] = nc.dram_tensor("WkvT", [D_MODEL, 2048], F32, kind="ExternalInput").ap()
    d["WprojT"] = nc.dram_tensor("WprojT", [D_MODEL, 1024], F32, kind="ExternalInput").ap()
    d["bprojT"] = nc.dram_tensor("bprojT", [128, 8], F32, kind="ExternalInput").ap()
    d["yT"] = nc.dram_tensor("yT", [D_MODEL, T_CORE], F32, kind="ExternalOutput").ap()
    d["cc_in"] = nc.dram_tensor("cc_in", [128, 520], F32).ap()
    d["cc_out"] = nc.dram_tensor("cc_out", [128, 520], F32).ap()

    with ExitStack() as ctx:
        p = Prog(nc, ctx)
        pool = [ctx.enter_context(nc.psum_tensor(f"pp{i}", [128, 512], F32)) for i in range(8)]
        s = attn_sbufs(nc, ctx)
        for c in range(NCH):
            p.dma("sync", s["X"][:, c * T_CORE:(c + 1) * T_CORE],
                  d["xT"][c * 128:(c + 1) * 128, :], ch="in")
        p.op("vector", lambda v: v.memset(s["ones_col"], 1.0), inc="sV")
        p.op("vector", lambda v: v.memset(s["ones_row"], 1.0), inc="sV")
        p.op("vector", lambda v: v.memset(s["ones64_bf"], 1.0), inc="sV")
        cp = ConstPool(nc, ctx, p)
        ev_c = cp.get(LN_EPS)
        ev_in = p.total("in")
        emit_attn(p, nc, pool, s, d, store_out=True, ev_ready=[ev_c, ev_in])
        block = ctx.enter_context(nc.Block())
        p.finish(block)
    return nc


def attn_in_map(xs, W):
    return dict(
        xT=xs,
        WqT=np.ascontiguousarray(W["Wqkv"][0:1024].T),
        WkvT=np.ascontiguousarray(W["Wqkv"][1024:3072].T),
        WprojT=np.ascontiguousarray(W["Wproj"].T),
        bprojT=np.ascontiguousarray(W["bproj"].reshape(8, 128).T),
    )


MB_NH = 32
MB_HP = 64
MB_NS = 128
MB_NC = 16


def mamba_scalar_math(p, s, cp, ev_scl, ev_ang):
    Q = s["SCLR"]
    D = s["DERIV"]
    SC = s["SCLT"]
    dtr, Ar, trap = Q[0:32, :], Q[32:64, :], Q[64:96, :]
    dt, a_, lam = D[0:32, :], D[32:64, :], D[64:96, :]
    w_ = Q[0:32, :]
    scr0 = SC[0:32, :]
    scr32 = s["ANGQ"][32:64, :]
    vt, wvm1, capr = SC[0:32, :], SC[32:64, :], SC[64:96, :]
    ev = {}
    e0 = p.op("vector", lambda v: v.tensor_scalar(dt, dtr, s["dtb"][0:32, 0:1], None, ALU.add),
              waits=[ev_scl, s["ev_dtb"], s["ev_memset"]], inc="sV")
    e1a = p.op("vector", lambda v: v.tensor_scalar(scr0, dt, -1.0, None, ALU.mult),
               waits=[e0], inc="sV")
    e1 = p.op("vector", lambda v: v.tensor_tensor(scr0, scr0, dt, ALU.max),
              waits=[e1a], inc="sV")
    e2 = p.op("scalar", lambda sc: sc.activation(scr0, scr0, AF.Exp, scale=-1.0),
              waits=[e1], inc="sS")
    e3 = p.op("scalar", lambda sc: sc.activation(scr0, scr0, AF.Ln, bias=1.0),
              waits=[e2, cp.get(1.0)], inc="sS")
    e4 = p.op("vector", lambda v: v.tensor_scalar(dt, dt, 0.0, None, ALU.max),
              waits=[e3, e0], inc="sV")
    e5 = p.op("vector", lambda v: v.tensor_tensor(dt, dt, scr0, ALU.add),
              waits=[e4], inc="sV")
    ev["dt"] = e5
    f1a = p.op("vector", lambda v: v.tensor_scalar(scr32, Ar, -1.0, None, ALU.mult),
               waits=[ev_scl, e5], inc="sV")
    f1 = p.op("vector", lambda v: v.tensor_tensor(scr32, scr32, Ar, ALU.max),
              waits=[f1a], inc="sV")
    f2 = p.op("scalar", lambda sc: sc.activation(scr32, scr32, AF.Exp, scale=-1.0),
              waits=[f1], inc="sS")
    f3 = p.op("scalar", lambda sc: sc.activation(scr32, scr32, AF.Ln, bias=1.0),
              waits=[f2], inc="sS")
    f4 = p.op("vector", lambda v: v.tensor_scalar(a_, Ar, 0.0, None, ALU.max),
              waits=[ev_scl, f3], inc="sV")
    f5 = p.op("vector", lambda v: v.tensor_tensor(a_, a_, scr32, ALU.add),
              waits=[f4], inc="sV")
    f6 = p.op("vector", lambda v: v.tensor_scalar(a_, a_, 1e-4, -1.0, ALU.max, ALU.mult),
              waits=[f5], inc="sV")
    fA = p.op("vector", lambda v: v.tensor_copy(scr32, dt), waits=[f6, e5], inc="sV")
    f7 = p.op("vector", lambda v: v.tensor_tensor(a_, a_, scr32, ALU.mult),
              waits=[fA], inc="sV")
    f8 = p.op("scalar", lambda sc: sc.activation(a_, a_, AF.Exp), waits=[f7], inc="sS")
    ev["a"] = f8
    g1 = p.op("scalar", lambda sc: sc.activation(lam, trap, AF.Sigmoid),
              waits=[ev_scl, f8], inc="sS")
    gA = p.op("vector", lambda v: v.tensor_copy(scr0, lam), waits=[g1, e5], inc="sV")
    g2 = p.op("vector", lambda v: v.tensor_tensor(w_, dt, scr0, ALU.mult),
              waits=[gA], inc="sV")
    gD = p.op("vector", lambda v: v.tensor_tensor(
        s["ANGQ"][64:96, 0:1], dt[:, 0:1], w_[:, 0:1], ALU.subtract),
        waits=[g2], inc="sV")
    ev["dtl1col"] = gD
    g4 = p.op("vector", lambda v: v.tensor_copy(vt, w_), waits=[gD, s["ev_memset"]], inc="sV")
    g5a = p.op("vector", lambda v: v.tensor_tensor(
        vt[:, 0:T_CORE - 1], vt[:, 0:T_CORE - 1], dt[:, 1:T_CORE], ALU.add),
        waits=[g4], inc="sV")
    g5 = p.op("vector", lambda v: v.tensor_tensor(
        vt[:, 0:T_CORE - 1], vt[:, 0:T_CORE - 1], w_[:, 1:T_CORE], ALU.subtract),
        waits=[g5a], inc="sV")
    ev["vt"] = g5
    gB = p.op("vector", lambda v: v.tensor_copy(wvm1, w_), waits=[g5], inc="sV")
    g6 = p.op("vector", lambda v: v.tensor_scalar(scr32, vt, 1e-30, None, ALU.max),
              waits=[g5, f7], inc="sV")
    g6b = p.op("vector", lambda v: v.reciprocal(scr32, scr32), waits=[g6], inc="sV")
    g7 = p.op("vector", lambda v: v.tensor_tensor(wvm1, wvm1, scr32, ALU.mult),
              waits=[g6b, gB], inc="sV")
    g8 = p.op("vector", lambda v: v.tensor_scalar(wvm1, wvm1, -1.0, None, ALU.add),
              waits=[g7], inc="sV")
    ev["wvm1"] = g8
    last = f8
    for c in range(MB_NC):
        sl = slice(c * 128, (c + 1) * 128)
        last = p.op("vector", lambda v, sl=sl: v.tensor_tensor_scan(
            capr[:, sl], a_[:, sl], s["zeros64"][32:64, 0:128], 1.0, ALU.mult, ALU.add),
            waits=[f8, s["ev_memset"]] + ([last] if last else []), inc="sV")
    ev["capr"] = last
    return ev


def mamba_sbufs1(nc, ctx):
    mk = lambda name, shape, dt=F32: ctx.enter_context(nc.sbuf_tensor(name, shape, dt))[:, :]
    s = {}
    arena = mk("arena_m", [128, 32768], BF16)
    s["X"] = arena.bitcast(F32)[:, 0:NCH * T_CORE]
    s["Ball"] = arena[:, 0:16384]
    s["Call"] = arena[:, 16384:32768]
    Xn = mk("Xn_m", [128, 16384], BF16)
    s["Xn"] = Xn
    XnF = Xn.bitcast(F32)
    s["xg"] = [Xn[:, 0:2048], Xn[:, 2048:4096]]
    s["xtl"] = Xn[:, 4096:6144]
    s["Escr"] = [XnF[:, 3072:3200], XnF[:, 3200:3328]]
    s["BdeT"] = [Xn[:, 6656:6784], Xn[:, 6784:6912]]
    s["eqc"] = XnF[:, 3456:3968]
    s["ahm"] = XnF[0:1, 4096:8192]
    wb = mk("Wbig", [128, 16384], BF16)
    s["Wbig"] = wb
    s["costt"] = wb[:, 0:4096]
    s["sintt"] = wb[:, 4096:8192]
    s["rp1"] = wb[:, 8192:8704]
    s["rp2"] = wb[:, 8704:9216]
    s["rpI32"] = wb[:, 8192:9216].bitcast(mybir.dt.int32)
    s["cos3"] = wb[:, 9216:9728]
    s["sin3"] = wb[:, 9728:10240]
    s["SCLR"] = mk("SCLR", [128, 2048])
    s["ANGQ"] = mk("ANGQ", [128, 2048])
    s["DERIV"] = mk("DERIV", [128, 2048])
    s["SCLT"] = mk("SCLT", [128, 2048])
    s["Spub"] = s["SCLT"].bitcast(BF16)[:, 0:2048]
    s["SCLT_tm"] = mk("SCLT_tm", [128, 2048])
    s["Bn"] = mk("Bn", [128, 2048])
    s["Cn"] = mk("Cn", [128, 2048])
    s["sq2"] = s["Bn"][:, 0:1024]
    s["xnt"] = s["Bn"][:, 1024:1536]
    s["stats"] = s["Cn"][0:1, 0:2048]
    s["S_all"] = mk("S_all", [128, 2048])
    s["ones_col"] = mk("ones_col_m", [128, 1])
    s["ones_row"] = mk("ones_row_m", [1, 128])
    s["onesf_row"] = mk("onesf_row_m", [1, 128])
    s["zeros64"] = mk("zeros64", [64, 128])
    s["dtb"] = mk("dtb_s", [32, 1])
    s["BbT"] = mk("BbT_s", [128, 64])
    s["Bw"] = mk("Bw_s", [128, 2])
    s["selg"] = mk("selg_s", [32, 1024])
    s["rep4"] = mk("rep4_s", [32, 128])
    s["sel96"] = mk("sel96_s", [128, 32], BF16)
    s["eye_f"] = mk("eye_f", [128, 128])
    s["eye_bf"] = mk("eye_bf", [128, 128], BF16)
    s["thcar"] = mk("thcar", [128, 8])
    s["wscr"] = mk("wscr", [128, 1024])
    SCLR = s["SCLR"]
    s["md"] = SCLR[:, 0:512]
    s["ones512"] = SCLR[:, 512:1024]
    s["angrep"] = SCLR[:, 1024:1536]
    s["useg"] = SCLR[:, 1536:2048]
    s["zstage"] = SCLR.bitcast(BF16)[:, 0:4096]
    s["thm"] = mk("thm", [128, 4])
    s["ce"] = mk("ce", [32, 32])
    s["blst"] = mk("blst", [128, 32], BF16)
    s["rowscr"] = mk("rowscr", [1, 128])
    return s


def emit_mamba_scl_inproj(p, pool, s, d, ev_ready, cp):
    X, Xn = s["X"], s["Xn"]
    ev_ln = emit_ln(p, pool, (0, 1, 2, 3), X, Xn, s["sq2"], s["xnt"], s["ones_col"],
                    s["ones_row"], s["stats"], T_CORE, list(ev_ready))
    wscl = s["Wbig"][:, 15360:16384]
    ev_w = p.dma("gpsimd", wscl.rearrange("p (c k) -> p c k", c=NCH),
                 d["W_in"].rearrange("(c p) f -> p c f", p=128)[:, :, 4352:4480], ch="wscl")
    ev_scl = None
    ev_ang = None
    for ti in range(4):
        t0 = ti * 512
        m1 = m2 = None
        for dc in range(NCH):
            m1 = p.op("tensor", lambda t, dc=dc, t0=t0: t.matmul(
                pool[4][0:96, :], wscl[:, dc * 128: dc * 128 + 96],
                Xn[:, dc * T_CORE + t0: dc * T_CORE + t0 + 512],
                start=(dc == 0), stop=(dc == NCH - 1)),
                waits=([ev_w, ev_ln[ti]] + ([ev_scl] if ev_scl else [])) if dc == 0 else [],
                inc="sT")
        for dc in range(NCH):
            m2 = p.op("tensor", lambda t, dc=dc, t0=t0: t.matmul(
                pool[5][0:32, :], wscl[:, dc * 128 + 96: dc * 128 + 128],
                Xn[:, dc * T_CORE + t0: dc * T_CORE + t0 + 512],
                start=(dc == 0), stop=(dc == NCH - 1)),
                waits=([ev_ang] if ev_ang and dc == 0 else []), inc="sT")
        ev_scl = p.op("vector", lambda v, t0=t0: v.tensor_copy(
            s["SCLR"][0:96, t0:t0 + 512], pool[4][0:96, :]), waits=[m1], inc="sV")
        ev_ang = p.op("vector", lambda v, t0=t0: v.tensor_copy(
            s["ANGQ"][0:32, t0:t0 + 512], pool[5][0:32, :]), waits=[m2], inc="sV")
    ev = mamba_scalar_math(p, s, cp, ev_scl, ev_ang)
    ev["ln"] = ev_ln
    ev["scl_mm"] = ev_scl
    ev["ang"] = ev_ang
    return ev


def emit_theta_tile(p, pool, s, ti, ev_sc, ev_prev_trig, cp):
    t0 = ti * 512
    ang = s["ANGQ"][0:32, t0:t0 + 512]
    dt = s["DERIV"][0:32, t0:t0 + 512]
    evs = []
    ev_ar_mm = p.op("tensor", lambda t: t.matmul(
        pool[6][:, :], s["rep4"][:, :], ang, start=True, stop=True),
        waits=[ev_sc, s["ev_consts"], ev_prev_trig], inc="sT")
    ev_ar = p.op("vector", lambda v: v.tensor_copy(s["angrep"], pool[6][:, :]),
                 waits=[ev_ar_mm], inc="sV")
    last = ev_ar
    for g in range(8):
        mg = p.op("tensor", lambda t, g=g: t.matmul(
            pool[7][:, :], s["selg"][:, g * 128:(g + 1) * 128], dt,
            start=True, stop=True), waits=[ev_sc, last], inc="sT")
        eu = p.op("vector", lambda v: v.tensor_tensor(s["useg"], s["angrep"], pool[7][:, :],
                                                      ALU.mult),
                  waits=[mg, ev_ar, last], inc="sV")
        th = s["wscr"][:, 0:512]
        esc = p.op("vector", lambda v, g=g, th=th: v.tensor_tensor_scan(
            th, s["ones512"], s["useg"], s["thcar"][:, g:g + 1],
            ALU.mult, ALU.add), waits=[eu, s["ev_ones512"]], inc="sV")
        ecar = p.op("vector", lambda v, g=g, th=th: v.tensor_copy(
            s["thcar"][:, g:g + 1], th[:, 511:512]), waits=[esc], inc="sV")
        md = s["md"]
        qi = s["rpI32"]
        es = None
        BK = 4096.0
        for qq, phi0 in ((0, np.pi / 2), (1, 0.0)):
            dst = (s["costt"] if qq == 0 else s["sintt"])[:, g * 512:(g + 1) * 512]
            w1_ = p.op("vector", lambda v, th=th, md=md, phi0=phi0: v.tensor_scalar(
                md, th, phi0, None, ALU.add), waits=[esc, ecar, es], inc="sV")
            w2_ = p.op("vector", lambda v, md=md: v.tensor_scalar(
                s["useg"], md, 1.0 / (2 * np.pi), BK, ALU.mult, ALU.add),
                waits=[w1_], inc="sV")
            w3_ = p.op("vector", lambda v, qi=qi: v.tensor_copy(qi, s["useg"]),
                       waits=[w2_], inc="sV")
            w4_ = p.op("vector", lambda v, qi=qi: v.tensor_copy(s["useg"], qi),
                       waits=[w3_], inc="sV")
            w4b = p.op("vector", lambda v: v.tensor_scalar(
                s["useg"], s["useg"], -BK, None, ALU.add), waits=[w4_], inc="sV")
            w5_ = p.op("vector", lambda v, md=md: v.scalar_tensor_tensor(
                md, s["useg"], -2 * np.pi, md, ALU.mult, ALU.add), waits=[w4b], inc="sV")
            wf1 = p.op("vector", lambda v, md=md: v.tensor_scalar(
                s["useg"], md, np.pi, None, ALU.is_gt), waits=[w5_], inc="sV")
            wf2 = p.op("vector", lambda v, md=md: v.scalar_tensor_tensor(
                md, s["useg"], -2 * np.pi, md, ALU.mult, ALU.add), waits=[wf1], inc="sV")
            w6_ = p.op("vector", lambda v, md=md: v.tensor_scalar(
                md, md, -3.14159, 3.14159, ALU.max, ALU.min), waits=[wf2], inc="sV")
            es = p.op("scalar", lambda sc, dst=dst, md=md: sc.activation(
                dst, md, AF.Sin), waits=[w6_], inc="sS")
        last = es
        evs.append(es)
    return evs[-1]


def emit_mamba1(p, nc, pool, s, d, cp, ev_ready):
    X, Xn = s["X"], s["Xn"]
    Wb = s["Wbig"]
    ev = emit_mamba_scl_inproj(p, pool, s, d, ev_ready, cp)
    ev_ln = ev["ln"]

    p.dma("sync", s["wscr"][:, 0:32].rearrange("p (j k) -> p j k", j=4),
          d["ThAll"].rearrange("j p k -> p j k"), ch="sc")
    p.dma("sync", s["thm"], d["thmask"], ch="sc")
    evh = evm = p.total("sc")
    e_t0 = p.op("vector", lambda v: v.memset(s["thcar"], 0.0), inc="sV")
    last = e_t0
    for j in range(4):
        last = p.op("vector", lambda v, j=j: v.scalar_tensor_tensor(
            s["thcar"], s["wscr"][:, j * 8:(j + 1) * 8], s["thm"][:, j:j + 1],
            s["thcar"], ALU.mult, ALU.add),
            waits=[evh, evm, last], inc="sV")
    ev_thoff = last

    ev_ce1 = p.op("vector", lambda v: v.tensor_copy(
        s["ce"][0:32, 0:16], s["SCLT"][64:96, 127:2048:128]), waits=[ev["capr"]], inc="sV")
    ev_ce2 = p.op("vector", lambda v: v.tensor_tensor_scan(
        s["ce"][0:32, 16:32], s["ce"][0:32, 0:16], s["zeros64"][0:32, 0:16], 1.0,
        ALU.mult, ALU.add), waits=[ev_ce1, s["ev_memset"]], inc="sV")
    ev["ceR"] = ev_ce2
    ev["aW"] = p.dma("sync", d["ascr"], s["DERIV"][32:64, :], waits=[ev["a"]], ch="aw1")
    p.dma("sync", d["caprf"], s["SCLT"][64:96, :], waits=[ev["capr"]], ch="aw2")
    ev_tmT = None
    for c in range(MB_NC):
        mt = p.op("tensor", lambda t, c=c: t.matmul(
            pool[6][:, 0:128], s["SCLT"][:, c * 128:(c + 1) * 128], s["eye_f"],
            start=True, stop=True, is_transpose=True),
            waits=[ev["capr"], ev["vt"], ev["wvm1"], s["ev_consts"], ev_tmT],
            inc="sT")
        ev_tmT = p.op("vector", lambda v, c=c: v.tensor_copy(
            s["SCLT_tm"][:, c * 128:(c + 1) * 128], pool[6][:, 0:128]), waits=[mt], inc="sV")
    ev["tmT"] = ev_tmT
    edc = ev["dtl1col"]
    mdt = p.op("tensor", lambda t: t.matmul(
        pool[7][0:1, 0:32], s["ANGQ"][64:96, 0:1], s["eye_f"][64:96, 64:96],
        start=True, stop=True, is_transpose=True),
        waits=[edc, ev_tmT], inc="sT")
    edt = p.op("vector", lambda v: v.tensor_copy(s["rowscr"][0:1, 0:32], pool[7][0:1, 0:32]),
               waits=[mdt], inc="sV")
    ev["dtl1row"] = edt
    p.dma("sync", d["scal2"], s["rowscr"][0:1, 0:32], waits=[edt], ch="aw3")
    ev_s0 = p.op("vector", lambda v: v.memset(s["S_all"], 0.0), inc="sV")


    ev_zw = p.dma("gpsimd", Wb.rearrange("p (c f) -> p c f", c=NCH),
                  d["W_in"].rearrange("(c p) f -> p c f", p=128)[:, :, 0:2048],
                  waits=[ev["scl_mm"], ev["ang"]], ch="wz")
    ev_zst = {0: ev_ln[-1], 1: ev_ln[-1], 2: ev_ln[-1], 3: ev_ln[-1]}
    ev_zsd = {0: None, 1: None, 2: None, 3: None}
    for g in range(16):
        t0g = g * 128
        for cb in range(4):
            bk = cb
            zs = s["zstage"][:, cb * 512:(cb + 1) * 512]
            mm = None
            for dc in range(NCH):
                mm = p.op("tensor", lambda t, dc=dc, t0g=t0g, cb=cb, bk=bk: t.matmul(
                    pool[bk][:, :],
                    Xn[:, dc * T_CORE + t0g: dc * T_CORE + t0g + 128],
                    Wb[:, dc * 2048 + cb * 512: dc * 2048 + cb * 512 + 512],
                    start=(dc == 0), stop=(dc == NCH - 1)),
                    waits=([ev_zw, ev_ln[g // 4], ev_zst[bk], ev_thoff] if dc == 0 else []),
                    inc="sT")
            eva = p.op("scalar", lambda sc, bk=bk, zs=zs: sc.activation(
                zs, pool[bk][:, :], AF.Silu),
                waits=[mm, ev_zsd[cb]], inc="sS")
            ev_zst[bk] = eva
            ev_zsd[cb] = p.dma("sync", d["zscr"][g, :, cb * 512:(cb + 1) * 512],
                               zs, waits=[eva], ch=f"zs{cb}")
    ev_xw = p.dma("gpsimd", Wb.rearrange("p (c f) -> p c f", c=NCH),
                  d["W_in"].rearrange("(c p) f -> p c f", p=128)[:, :, 2048:4096],
                  waits=[ev_zst[0], ev_zst[1], ev_zst[2], ev_zst[3]], ch="wz")
    ev_xst = dict(ev_zst)
    ev_xsd = dict(ev_zsd)
    for g in range(16):
        t0g = g * 128
        for cb in range(4):
            bk = cb
            zs = s["zstage"][:, cb * 512:(cb + 1) * 512]
            mm = None
            for dc in range(NCH):
                mm = p.op("tensor", lambda t, dc=dc, t0g=t0g, cb=cb, bk=bk: t.matmul(
                    pool[bk][:, :],
                    Xn[:, dc * T_CORE + t0g: dc * T_CORE + t0g + 128],
                    Wb[:, dc * 2048 + cb * 512: dc * 2048 + cb * 512 + 512],
                    start=(dc == 0), stop=(dc == NCH - 1)),
                    waits=([ev_xw, ev_xst[bk]] if dc == 0 else []), inc="sT")
            eva = p.op("vector", lambda v, bk=bk, zs=zs: v.tensor_copy(
                zs, pool[bk][:, :]),
                waits=[mm, ev_xsd[cb]], inc="sV")
            ev_xst[bk] = eva
            ev_xsd[cb] = p.dma("sync", d["xscr"][g, :, cb * 512:(cb + 1) * 512],
                               zs, waits=[eva], ch=f"xs{cb}")
    ev_xdone = [p.total(f"xs{cb}") for cb in range(4)]

    s["ev_ones512"] = p.op("vector", lambda v: v.memset(s["ones512"], 1.0),
                           waits=ev_xdone, inc="sV")
    wbc = Wb[:, 10240:12288]
    ev_bcw = p.dma("gpsimd", wbc.rearrange("p (c f) -> p c f", c=NCH),
                   d["W_in"].rearrange("(c p) f -> p c f", p=128)[:, :, 4096:4352],
                   waits=ev_xdone, ch="wz")
    ev_bc = {("sq",): ev["tmT"], ("b7",): ev["dtl1row"]}
    for ti in range(4):
        t0 = ti * 512
        for bc in range(2):
            bk = 4 + bc
            mm = None
            for dc in range(NCH):
                mm = p.op("tensor", lambda t, dc=dc, t0=t0, bc=bc, bk=bk: t.matmul(
                    pool[bk][:, :],
                    wbc[:, dc * 256 + bc * 128: dc * 256 + bc * 128 + 128],
                    Xn[:, dc * T_CORE + t0: dc * T_CORE + t0 + 512],
                    start=(dc == 0), stop=(dc == NCH - 1)),
                    waits=([ev_bcw, ev_bc.get((ti - 1, bc))] if dc == 0 else []), inc="sT")
            sq = s["wscr"][:, 0:512]
            e1 = p.op("scalar", lambda sc, bk=bk: sc.activation(sq, pool[bk][:, :], AF.Square),
                      waits=[mm, ev_bc.get(("sq",))], inc="sS")
            m2 = p.op("tensor", lambda t: t.matmul(
                pool[6][0:1, :], s["ones_col"], sq, start=True, stop=True),
                waits=[e1, ev_bc.get(("b6",))], inc="sT")
            ev_bc[("sq",)] = m2
            nr = s["wscr"][0:1, 512:1024]
            e3 = p.op("scalar", lambda sc: sc.mul(nr, pool[6][0:1, :], 1.0 / 128.0),
                      waits=[m2], inc="sS")
            e4 = p.op("scalar", lambda sc: sc.activation(nr, nr, AF.Ln, bias=LN_EPS),
                      waits=[e3], inc="sS")
            e5 = p.op("scalar", lambda sc: sc.activation(nr, nr, AF.Exp, scale=-0.5),
                      waits=[e4], inc="sS")
            ev_bc[("b6",)] = e3
            m3 = p.op("tensor", lambda t: t.matmul(pool[7][:, :], s["ones_row"], nr,
                                                   start=True, stop=True),
                      waits=[e5, ev_bc.get(("b7",))], inc="sT")
            zrb = s["zstage"][:, 0:512]
            e5c = p.op("vector", lambda v: v.tensor_copy(zrb, pool[7][:, :]),
                       waits=[m3, ev_bc.get(("zrb",))], inc="sV")
            dst = (s["Bn"] if bc == 0 else s["Cn"])[:, t0:t0 + 512]
            e6 = p.op("vector", lambda v, bk=bk, bc=bc, dst=dst, zrb=zrb: v.scalar_tensor_tensor(
                dst, pool[bk][:, :], s["Bw"][:, bc:bc + 1], zrb,
                ALU.mult, ALU.mult),
                waits=[e5c, mm, s["ev_consts"]], inc="sV")
            ev_bc[("zrb",)] = e6
            ev_bc[(ti, bc)] = e6
            ev_bc[("b7",)] = e6

    ev_prev_trig = (ev_bc[(3, 1)][0], max(ev_bc[(3, 1)][1], ev_tmT[1]))
    ev_sch = {k: None for k in range(8)}
    ev_sall = None
    ev_sst_prev = None
    ev_xld = {0: None, 1: None}
    ev_bcout = {0: None, 1: None}
    for ti in range(4):
        t0 = ti * 512
        ev_trig = emit_theta_tile(p, pool, s, ti, ev["a"], ev_prev_trig, cp)
        ev_prev_trig = ev_trig
        ev_rope = None
        S1, S2 = s["cos3"], s["sin3"]
        ev_s1use = None
        for h in range(32):
            g, hb = h // 4, h % 4
            gsl = slice(g * 512, (g + 1) * 512)
            stg = []
            if hb == 3:
                mc = p.op("tensor", lambda t, gsl=gsl: t.matmul(
                    pool[6][0:32, 0:512], s["sel96"], s["costt"][:, gsl],
                    start=True, stop=True), waits=[ev_trig, ev_sch.get("c3"), s["ev_constsG"]],
                    inc="sT")
                stg.append(p.op("vector", lambda v: v.tensor_copy(
                    S1[0:32, :], pool[6][0:32, 0:512]),
                    waits=[mc, ev_s1use], inc="sV"))
                stg.append(p.op("vector", lambda v: v.tensor_copy(
                    S2[32:64, :], pool[6][0:32, 0:512]),
                    waits=[mc, stg[0]], inc="sV"))
                ms = p.op("tensor", lambda t, gsl=gsl: t.matmul(
                    pool[7][0:32, 0:512], s["sel96"], s["sintt"][:, gsl],
                    start=True, stop=True), waits=[ev_trig, stg[1]], inc="sT")
                ev_sch["c3"] = ms
                stg.append(p.op("vector", lambda v: v.tensor_copy(
                    S1[32:64, :], pool[7][0:32, 0:512]),
                    waits=[ms], inc="sV"))
                stg.append(p.op("vector", lambda v: v.tensor_copy(
                    S2[0:32, :], pool[7][0:32, 0:512]),
                    waits=[ms, stg[2]], inc="sV"))
            else:
                r0 = hb * 32
                stg.append(p.op("vector", lambda v, r0=r0, gsl=gsl: v.tensor_copy(
                    S1[0:32, :], s["costt"][r0:r0 + 32, gsl]),
                    waits=[ev_trig, ev_s1use], inc="sV"))
                stg.append(p.op("vector", lambda v, r0=r0, gsl=gsl: v.tensor_copy(
                    S1[32:64, :], s["sintt"][r0:r0 + 32, gsl]),
                    waits=[ev_trig, stg[0]], inc="sV"))
                stg.append(p.op("vector", lambda v, r0=r0, gsl=gsl: v.tensor_copy(
                    S2[0:32, :], s["sintt"][r0:r0 + 32, gsl]),
                    waits=[stg[1]], inc="sV"))
                stg.append(p.op("vector", lambda v, r0=r0, gsl=gsl: v.tensor_copy(
                    S2[32:64, :], s["costt"][r0:r0 + 32, gsl]),
                    waits=[stg[2]], inc="sV"))
            ev_cs = stg[-1]
            for bc in range(2):
                src_n = s["Bn"] if bc == 0 else s["Cn"]
                dst_all = s["Ball"] if bc == 0 else s["Call"]
                dsl = dst_all[:, h * 512: h * 512 + 512]
                eb1 = p.op("vector", lambda v, h=h, bc=bc, dsl=dsl, src_n=src_n, t0=t0: v.tensor_scalar(
                    dsl, src_n[:, t0:t0 + 512], s["BbT"][:, bc * 32 + h: bc * 32 + h + 1],
                    None, ALU.add),
                    waits=[ev_bc[(ti, bc)], ev_rope, ev_cs]
                    + ([ev_bcout[0], ev_bcout[1], ev_sch.get("bt_last")] if h == 0 else []),
                    inc="sV")
                t1 = p.op("vector", lambda v, dsl=dsl: v.tensor_tensor(
                    s["rp1"][0:32, :], dsl[0:32, :], S1[0:32, :], ALU.mult), waits=[eb1], inc="sV")
                t2 = p.op("vector", lambda v, dsl=dsl: v.tensor_tensor(
                    s["rp2"][0:32, :], dsl[32:64, :], S1[32:64, :], ALU.mult), waits=[t1], inc="sV")
                t3 = p.op("vector", lambda v, dsl=dsl: v.tensor_tensor(
                    s["rp1"][32:64, :], dsl[0:32, :], S2[0:32, :], ALU.mult), waits=[t2], inc="sV")
                t4 = p.op("vector", lambda v, dsl=dsl: v.tensor_tensor(
                    s["rp2"][32:64, :], dsl[32:64, :], S2[32:64, :], ALU.mult), waits=[t3], inc="sV")
                t5 = p.op("vector", lambda v, dsl=dsl: v.tensor_tensor(
                    dsl[0:32, :], s["rp1"][0:32, :], s["rp2"][0:32, :], ALU.subtract),
                    waits=[t4], inc="sV")
                ev_rope = p.op("vector", lambda v, dsl=dsl: v.tensor_tensor(
                    dsl[32:64, :], s["rp1"][32:64, :], s["rp2"][32:64, :], ALU.add),
                    waits=[t5], inc="sV")
            ev_s1use = ev_rope
        ev_bcout[ti % 2] = p.dma("sync", d["Bscr"][ti], s["Ball"], waits=[ev_rope],
                                 ch=f"bco{ti % 2}")
        ev_bcout[ti % 2] = p.dma("sync", d["Cscr"][ti], s["Call"], waits=[ev_rope],
                                 ch=f"bco{ti % 2}")
        for cc in range(4):
            c = ti * 4 + cc
            par = c % 2
            ev_sst_prev = p.dma("sync", d["Sscr"][c], s["S_all"],
                                waits=[ev_sall if ev_sall else ev_s0, ev_sst_prev], ch="sst")
            if c == 0:
                ev_eqw = p.dma("sync", d["cescr"], s["ce"], waits=[ev["ceR"]], ch="ce1")
                ev_eqc = p.dma("sync", s["eqc"].rearrange("p (h k) -> p h k", h=32),
                               d["cescr"][:, 0:16].rearrange("(o h) k -> o h k", o=1
                                                             ).partition_broadcast(128),
                               waits=[ev_eqw], ch="ce2")
            ev_xld[par] = p.dma("sync", s["xg"][par], d["xscr"][c],
                                waits=[ev_xld[par], ev_sch.get(("xtl", par))] + ev_xdone,
                                ch=f"xl{par}")
            ev_ah = p.dma("sync", s["ahm"].rearrange("o (h i) -> o h i", h=32),
                          d["ascr"].rearrange("h (c i) -> c h i", i=128)[c:c + 1, :, :],
                          waits=[ev_sch.get("ahm"), ev["aW"]], ch="ah")
            x3 = s["xg"][par].rearrange("p (h e) -> p h e", h=32)
            vt3 = s["SCLT_tm"][:, c * 128: c * 128 + 32].rearrange(
                "p (h o) -> p h o", o=1).to_broadcast([128, 32, 64])
            ev_xtl = p.op("vector", lambda v, x3=x3, vt3=vt3: v.tensor_tensor(
                s["xtl"].rearrange("p (h e) -> p h e", h=32), x3, vt3, ALU.mult),
                waits=[ev_xld[par], ev["tmT"], ev_sch.get("xtl_use")], inc="sV")
            ev_sch[("xtl", par)] = ev_xtl
            for h in range(32):
                ba = (h % 2)
                bb = 2 + (h % 2)
                bs = 4 + (h % 2)
                pe = h % 2
                ma = p.op("tensor", lambda t, h=h, ba=ba: t.matmul(
                    pool[ba][:, 0:128], s["onesf_row"], s["ahm"][0:1, h * 128:(h + 1) * 128],
                    start=True, stop=True),
                    waits=[ev_ah, ev_sch[ba], s["ev_memset"]], inc="sT")
                ev_sch["ahm"] = ma
                esc = p.op("vector", lambda v, ba=ba, pe=pe: v.tensor_tensor_scan(
                    s["Escr"][pe], pool[ba][:, 0:128], s["eye_f"], 0.0, ALU.mult, ALU.add),
                    waits=[ma, ev_sch[("E", pe)] if ("E", pe) in ev_sch else None,
                           s["ev_consts"]], inc="sV")
                ev_sch[ba] = esc
                mt = p.op("tensor", lambda t, h=h, cc=cc, bb=bb: t.matmul(
                    pool[bb].bitcast(BF16)[:, 0:128],
                    s["Ball"][:, h * 512 + cc * 128: h * 512 + cc * 128 + 128],
                    s["eye_bf"], start=True, stop=True, is_transpose=True),
                    waits=[ev_sch[bb], s["ev_constsG"]], inc="sT")
                ev_sch["bt_last"] = mt
                ebd = p.op("vector", lambda v, bb=bb, pe=pe: v.tensor_scalar(
                    s["BdeT"][pe], pool[bb].bitcast(BF16)[:, 0:128],
                    s["Escr"][pe][:, 127:128], None, ALU.mult),
                    waits=[mt, esc, ev_sch.get(("bd", pe))], inc="sV")
                ev_sch[bb] = ebd
                ev_sch[("E", pe)] = ebd
                ms = p.op("tensor", lambda t, h=h, bs=bs, pe=pe: t.matmul(
                    pool[bs][:, 0:64], s["BdeT"][pe],
                    s["xtl"][:, h * 64:(h + 1) * 64], start=True, stop=True),
                    waits=[ebd, ev_xtl, ev_sch[bs]], inc="sT")
                esn = p.op("vector", lambda v, h=h, c=c, bs=bs: v.scalar_tensor_tensor(
                    s["S_all"][:, h * 64:(h + 1) * 64], s["S_all"][:, h * 64:(h + 1) * 64],
                    s["eqc"][:, h * 16 + c: h * 16 + c + 1], pool[bs][:, 0:64],
                    ALU.mult, ALU.add),
                    waits=[ms, ev_eqc, ev_sst_prev], inc="sV")
                ev_sch[bs] = esn
                ev_sch[("bd", pe)] = ms
                ev_sch["xtl_use"] = ms
                ev_sall = esn
    return ev, ev_sall, ev_sst_prev


def emit_mamba1_publish(p, s, d, ev_sall, ti_last_par):
    ev1 = p.op("vector", lambda v: v.tensor_copy(s["Spub"], s["S_all"]),
               waits=[ev_sall, p.total("aw2"), p.total("aw1"), p.total("aw3")], inc="sV")
    p.dma("sync", d["pubS"], s["Spub"], waits=[ev1], ch="pub")
    ev2 = p.op("vector", lambda v: v.tensor_copy(
        s["blst"], s["Ball"][:, 511:16384:512]), waits=[ev_sall], inc="sV")
    ev3 = p.op("tensor", lambda t: t.matmul(
        s["pool6"].bitcast(BF16)[0:32, 0:128], s["blst"], s["eye_bf"],
        start=True, stop=True, is_transpose=True), waits=[ev2], inc="sT")
    ev4 = p.op("vector", lambda v: v.tensor_copy(
        s["zstage"][0:32, 0:128], s["pool6"].bitcast(BF16)[0:32, 0:128]),
        waits=[ev3], inc="sV")
    p.dma("sync", d["pubBx"], s["zstage"][0:32, 0:128], waits=[ev4], ch="pub")
    p.dma("sync", d["pubXl"], s["xg"][ti_last_par][127:128, :], waits=[ev_sall], ch="pub")


def build_mamba1_program():
    nc = bass.Bass("TRN2", target_bir_lowering=False)
    d = {}
    d["xT"] = nc.dram_tensor("xT", [D_MODEL, T_CORE], F32, kind="ExternalInput").ap()
    d["W_in"] = nc.dram_tensor("W_in", [D_MODEL, 4480], F32, kind="ExternalInput").ap()
    d["dtb"] = nc.dram_tensor("dtb", [32, 1], F32, kind="ExternalInput").ap()
    d["BbT"] = nc.dram_tensor("BbT", [128, 64], F32, kind="ExternalInput").ap()
    d["Bw"] = nc.dram_tensor("Bw", [128, 2], F32, kind="ExternalInput").ap()
    d["selg"] = nc.dram_tensor("selg", [32, 1024], F32, kind="ExternalInput").ap()
    d["rep4"] = nc.dram_tensor("rep4", [32, 128], F32, kind="ExternalInput").ap()
    d["sel96"] = nc.dram_tensor("sel96", [128, 32], F32, kind="ExternalInput").ap()
    d["identD"] = nc.dram_tensor("identD", [128, 128], F32, kind="ExternalInput").ap()
    d["ThAll"] = nc.dram_tensor("ThAll", [4, 128, 8], F32, kind="ExternalInput").ap()
    d["thmask"] = nc.dram_tensor("thmask", [128, 4], F32, kind="ExternalInput").ap()
    d["zscr"] = nc.dram_tensor("zscr", [16, 128, 2048], BF16, kind="ExternalOutput").ap()
    d["xscr"] = nc.dram_tensor("xscr", [16, 128, 2048], BF16, kind="ExternalOutput").ap()
    d["Bscr"] = nc.dram_tensor("Bscr", [4, 128, 16384], BF16, kind="ExternalOutput").ap()
    d["Cscr"] = nc.dram_tensor("Cscr", [4, 128, 16384], BF16, kind="ExternalOutput").ap()
    d["Sscr"] = nc.dram_tensor("Sscr", [16, 128, 2048], F32, kind="ExternalOutput").ap()
    d["ascr"] = nc.dram_tensor("ascr", [32, 2048], F32, kind="ExternalOutput").ap()
    d["caprf"] = nc.dram_tensor("caprf", [32, 2048], F32, kind="ExternalOutput").ap()
    d["cescr"] = nc.dram_tensor("cescr", [32, 32], F32, kind="ExternalOutput").ap()
    d["scal2"] = nc.dram_tensor("scal2", [1, 32], F32, kind="ExternalOutput").ap()
    d["sctm"] = nc.dram_tensor("sctm", [128, 2048], F32, kind="ExternalOutput").ap()
    d["pubS"] = nc.dram_tensor("pubS", [128, 2048], BF16, kind="ExternalOutput").ap()
    d["pubBx"] = nc.dram_tensor("pubBx", [32, 128], BF16, kind="ExternalOutput").ap()
    d["pubXl"] = nc.dram_tensor("pubXl", [1, 2048], BF16, kind="ExternalOutput").ap()
    d["dbgBn"] = nc.dram_tensor("dbgBn", [128, 2048], F32, kind="ExternalOutput").ap()
    d["dbgCn"] = nc.dram_tensor("dbgCn", [128, 2048], F32, kind="ExternalOutput").ap()
    d["dbgcos"] = nc.dram_tensor("dbgcos", [128, 4096], BF16, kind="ExternalOutput").ap()
    d["dbgsin"] = nc.dram_tensor("dbgsin", [128, 4096], BF16, kind="ExternalOutput").ap()

    with ExitStack() as ctx:
        p = Prog(nc, ctx)
        pool = [ctx.enter_context(nc.psum_tensor(f"pp{i}", [128, 512], F32)) for i in range(8)]
        s = mamba_sbufs1(nc, ctx)
        s["pool6"] = pool[6]
        cp = ConstPool(nc, ctx, p)
        p.op("vector", lambda v: v.memset(s["ones_col"], 1.0), inc="sV")
        p.op("vector", lambda v: v.memset(s["ones_row"], 1.0), inc="sV")
        p.op("vector", lambda v: v.memset(s["onesf_row"], 1.0), inc="sV")
        p.op("vector", lambda v: v.memset(s["ones512"], 1.0), inc="sV")
        p.op("vector", lambda v: v.memset(s["zeros64"], 0.0), inc="sV")
        ev_ms = p.op("vector", lambda v: v.memset(s["SCLT"], 0.0), inc="sV")
        s["ev_memset"] = ev_ms
        for c in range(NCH):
            p.dma("sync", s["X"][:, c * T_CORE:(c + 1) * T_CORE],
                  d["xT"][c * 128:(c + 1) * 128, :], ch="in")
        p.dma("sync", s["dtb"], d["dtb"], ch="in")
        s["ev_dtb"] = p.total("in")
        p.dma("sync", s["BbT"], d["BbT"], ch="in")
        p.dma("sync", s["Bw"], d["Bw"], ch="in")
        p.dma("sync", s["selg"], d["selg"], ch="in")
        p.dma("sync", s["rep4"], d["rep4"], ch="in")
        p.dma("gpsimd", s["sel96"], d["sel96"], ch="inG")
        p.dma("sync", s["eye_f"], d["identD"], ch="in")
        p.dma("gpsimd", s["eye_bf"], d["identD"], ch="inG")
        ev_in = p.total("in")
        ev_inG = p.total("inG")
        s["ev_dtb"] = ev_in
        s["ev_consts"] = ev_in
        s["ev_constsG"] = ev_inG
        ev_c = cp.get(LN_EPS)
        cp.get(1.0)
        cp.get(-np.pi)
        ev, ev_sall, _ = emit_mamba1(p, nc, pool, s, d, cp,
                                     [ev_c, ev_in, ev_inG, ev_ms])
        p.dma("sync", d["sctm"], s["SCLT_tm"], waits=[ev["tmT"]], ch="aw4")
        p.dma("sync", d["dbgBn"], s["Bn"], waits=[ev_sall], ch="dbg")
        p.dma("sync", d["dbgCn"], s["Cn"], waits=[ev_sall], ch="dbg")
        p.dma("sync", d["dbgcos"], s["costt"], waits=[ev_sall], ch="dbg")
        p.dma("sync", d["dbgsin"], s["sintt"], waits=[ev_sall], ch="dbg")
        emit_mamba1_publish(p, s, d, ev_sall, 15 % 2)
        block = ctx.enter_context(nc.Block())
        p.finish(block)
    return nc


def build_mamba2_program():
    nc = bass.Bass("TRN2", target_bir_lowering=False)
    d = {}
    din = lambda n, shp, dt=F32: nc.dram_tensor(n, shp, dt, kind="ExternalInput").ap()
    d["xT"] = din("xT", [D_MODEL, T_CORE])
    d["W_out"] = din("W_out", [2048, 1024])
    d["Drow"] = din("Drow", [1, 32])
    d["identD"] = din("identD", [128, 128])
    d["xscr"] = din("xscr", [16, 128, 2048], BF16)
    d["zscr"] = din("zscr", [16, 128, 2048], BF16)
    d["Bscr"] = din("Bscr", [4, 128, 16384], BF16)
    d["Cscr"] = din("Cscr", [4, 128, 16384], BF16)
    d["Sscr"] = din("Sscr", [16, 128, 2048])
    d["ascr"] = din("ascr", [32, 2048])
    d["caprf"] = din("caprf", [32, 2048])
    d["cescr"] = din("cescr", [32, 32])
    d["scal2"] = din("scal2", [1, 32])
    d["sctm"] = din("sctm", [128, 2048])
    d["SAll"] = din("SAll", [4, 128, 2048], BF16)
    d["QtotAll"] = din("QtotAll", [32, 4])
    d["qmask"] = din("qmask", [32, 4])
    d["qmask1m"] = din("qmask1m", [32, 4])
    d["prevB"] = din("prevB", [1, 4096], BF16)
    d["prevXl"] = din("prevXl", [1, 2048], BF16)
    d["yT"] = nc.dram_tensor("yT", [D_MODEL, T_CORE], F32, kind="ExternalOutput").ap()

    with ExitStack() as ctx:
        p = Prog(nc, ctx)
        pool = [ctx.enter_context(nc.psum_tensor(f"pp{i}", [128, 512], F32)) for i in range(8)]
        mk2 = lambda name, shape, dt=F32: ctx.enter_context(nc.sbuf_tensor(name, shape, dt))[:, :]
        s = {}
        s["Wo"] = mk2("Wo", [128, 16384], BF16)
        s["Btt"] = mk2("Btt", [128, 16384], BF16)
        s["Ctt"] = mk2("Ctt", [128, 16384], BF16)
        s["SCLT_tm"] = mk2("SCLT_tm2", [128, 2048])
        s["eqc"] = mk2("eqc2", [128, 512])
        s["S0s"] = mk2("S0s", [128, 2048])
        s["Sl"] = mk2("Sl", [128, 2048])
        s["xg"] = [mk2(f"xg2{i}", [128, 2048], BF16) for i in range(2)]
        s["zg"] = [mk2(f"zg2{i}", [128, 2048], BF16) for i in range(2)]
        s["xtl"] = mk2("xtl2", [128, 2048], BF16)
        s["yl"] = mk2("yl", [128, 2048], BF16)
        s["yfm"] = mk2("yfm", [128, 2048], BF16)
        s["Escr"] = [mk2(f"Escr2{i}", [128, 128]) for i in range(2)]
        s["Mp"] = [mk2(f"Mp{i}", [128, 128], BF16) for i in range(2)]
        s["Cde"] = [mk2(f"Cde{i}", [128, 128], BF16) for i in range(2)]
        s["Sgb"] = [mk2(f"Sgb{i}", [128, 64], BF16) for i in range(2)]
        s["ahm"] = mk2("ahm2", [1, 4096])
        s["chm"] = mk2("chm2", [1, 4096])
        s["eye_f"] = mk2("eye_f2", [128, 128])
        s["eye_bf"] = mk2("eye_bf2", [128, 128], BF16)
        s["onesf_row"] = mk2("onesf_row2", [1, 128])
        s["Db"] = mk2("Db2", [128, 32])
        s["cwork"] = mk2("cwork", [32, 16])
        s["crow"] = mk2("crow", [1, 128])
        s["cb"] = mk2("cb", [128, 32])
        s["Sgrp"] = mk2("Sgrp", [128, 2048], BF16)
        s["xls"] = mk2("xls", [1, 2048], BF16)
        s["dtl1r"] = mk2("dtl1r", [1, 32])
        s["xres"] = mk2("xres2", [128, 256])
        s["ocb"] = mk2("ocb2", [128, 256])
        s["Ibf"] = s["eye_bf"]

        p.op("vector", lambda v: v.memset(s["onesf_row"], 1.0), inc="sV")
        ev_ms = p.op("vector", lambda v: v.memset(s["S0s"], 0.0), inc="sV")
        p.dma("sync", s["eye_f"], d["identD"], ch="in")
        p.dma("gpsimd", s["eye_bf"], d["identD"], ch="inG")
        p.dma("gpsimd", s["Wo"].rearrange("p (c f) -> p c f", c=16),
              d["W_out"].rearrange("(c p) f -> p c f", p=128), ch="inG")
        p.dma("sync", s["SCLT_tm"], d["sctm"], ch="in")
        p.dma("sync", s["eqc"].rearrange("p (h k) -> p h k", h=32),
              d["cescr"][:, 0:16].rearrange("(o h) k -> o h k", o=1).partition_broadcast(128),
              ch="in")
        p.dma("sync", s["dtl1r"], d["scal2"], ch="in")
        p.dma("gpsimd", s["xls"], d["prevXl"], ch="inG")
        p.dma("sync", s["cwork"][:, 0:4], d["QtotAll"], ch="in")
        p.dma("sync", s["cwork"][:, 4:8], d["qmask"], ch="in")
        p.dma("sync", s["cwork"][:, 8:12], d["qmask1m"], ch="in")
        ev_in = p.total("in")
        ev_inG = p.total("inG")
        cp = ConstPool(nc, ctx, p)

        evDr = p.dma("sync", s["crow"][0:1, 0:32], d["Drow"], ch="in2")
        mD = p.op("tensor", lambda t: t.matmul(pool[0][:, 0:32], s["onesf_row"],
                                               s["crow"][0:1, 0:32], start=True, stop=True),
                  waits=[evDr, ev_in, ev_ms], inc="sT")
        evDb = p.op("vector", lambda v: v.tensor_copy(s["Db"], pool[0][:, 0:32]),
                    waits=[mD], inc="sV")

        Q, M, M1 = s["cwork"][:, 0:4], s["cwork"][:, 4:8], s["cwork"][:, 8:12]
        eqm = s["cwork"][:, 12:16]
        e1 = p.op("vector", lambda v: v.tensor_tensor(eqm, Q, M, ALU.mult),
                  waits=[ev_in], inc="sV")
        e2 = p.op("vector", lambda v: v.tensor_tensor(eqm, eqm, M1, ALU.add),
                  waits=[e1], inc="sV")
        cc_ = s["cwork"][:, 0:4]
        ea = p.op("vector", lambda v: v.tensor_copy(cc_[:, 2:3], eqm[:, 3:4]), waits=[e2], inc="sV")
        eb = p.op("vector", lambda v: v.tensor_tensor(cc_[:, 1:2], cc_[:, 2:3], eqm[:, 2:3], ALU.mult),
                  waits=[ea], inc="sV")
        ec2 = p.op("vector", lambda v: v.tensor_tensor(cc_[:, 0:1], cc_[:, 1:2], eqm[:, 1:2], ALU.mult),
                   waits=[eb], inc="sV")
        ed = p.op("vector", lambda v: v.tensor_copy(cc_[:, 3:4], M[:, 3:4]), waits=[ec2], inc="sV")
        ee = p.op("vector", lambda v: v.tensor_tensor(cc_[:, 2:3], cc_[:, 2:3], M[:, 2:3], ALU.mult),
                  waits=[ed], inc="sV")
        ef = p.op("vector", lambda v: v.tensor_tensor(cc_[:, 1:2], cc_[:, 1:2], M[:, 1:2], ALU.mult),
                  waits=[ee], inc="sV")
        eg_ = p.op("vector", lambda v: v.tensor_tensor(cc_[:, 0:1], cc_[:, 0:1], M[:, 0:1], ALU.mult),
                   waits=[ef], inc="sV")
        ev_c = eg_
        ev_sacc = ev_ms
        ev_ld_prev = None
        for j in range(4):
            evl = p.dma("sync", s["Sgrp"], d["SAll"][j], waits=[ev_ld_prev, ev_sacc], ch="sg")
            ev_ld_prev = evl
            mt1 = p.op("tensor", lambda t, j=j: t.matmul(
                pool[1][0:1, 0:32], s["cwork"][0:32, j:j + 1], s["eye_f"][0:32, 0:32],
                start=True, stop=True, is_transpose=True),
                waits=[ev_c, ev_sacc], inc="sT")
            ev1 = p.op("vector", lambda v: v.tensor_copy(s["crow"][0:1, 0:32], pool[1][0:1, 0:32]),
                       waits=[mt1, evDb, evDr], inc="sV")
            mt2 = p.op("tensor", lambda t: t.matmul(
                pool[2][:, 0:32], s["onesf_row"], s["crow"][0:1, 0:32],
                start=True, stop=True), waits=[ev1], inc="sT")
            ev2 = p.op("vector", lambda v: v.tensor_copy(s["cb"], pool[2][:, 0:32]),
                       waits=[mt2], inc="sV")
            for h in range(32):
                ev_sacc = p.op("vector", lambda v, h=h: v.scalar_tensor_tensor(
                    s["S0s"][:, h * 64:(h + 1) * 64], s["Sgrp"][:, h * 64:(h + 1) * 64],
                    s["cb"][:, h:h + 1], s["S0s"][:, h * 64:(h + 1) * 64],
                    ALU.mult, ALU.add),
                    waits=[evl, ev2, ev_sacc], inc="sV")
        d3 = s["dtl1r"].rearrange("o (h k) -> o h k", k=1).to_broadcast([1, 32, 64])
        ev_x1 = p.op("vector", lambda v, d3=d3: v.tensor_tensor(
            s["xls"].rearrange("o (h e) -> o h e", h=32),
            s["xls"].rearrange("o (h e) -> o h e", h=32), d3, ALU.mult),
            waits=[ev_in, ev_inG, ev_sacc], inc="sV")
        evpb = p.dma("gpsimd", s["yfm"][0:1, 0:2048], d["prevB"][0:1, 0:2048], ch="pb")
        evpb2 = p.dma("gpsimd", s["yl"][0:1, 0:2048], d["prevB"][0:1, 2048:4096], ch="pb")
        evpb = evpb2 = p.total("pb")
        for h in range(32):
            rsl = (s["yfm"] if h < 16 else s["yl"])[0:1, (h % 16) * 128:(h % 16) * 128 + 128]
            mr = p.op("tensor", lambda t, h=h, rsl=rsl: t.matmul(
                pool[3][:, 0:64], rsl, s["xls"][0:1, h * 64:(h + 1) * 64],
                start=True, stop=True),
                waits=[evpb, evpb2, ev_x1, ev_sacc], inc="sT")
            ev_sacc = p.op("vector", lambda v, h=h: v.tensor_tensor(
                s["S0s"][:, h * 64:(h + 1) * 64], s["S0s"][:, h * 64:(h + 1) * 64],
                pool[3][:, 0:64], ALU.add),
                waits=[mr], inc="sV")
        ev_s0done = ev_sacc
        p.cnt_markers = None

        ev_sch = {k: None for k in range(8)}
        ev_sch["s0"] = ev_s0done
        ev_xld = {0: None, 1: None}
        ev_zld = {0: None, 1: None}
        ev_btld = {0: None, 1: None}
        ev_sl = None
        ev_ost = {0: None, 1: None}
        ev_yl_done = None
        ev_yfm_use = None
        for c in range(MB_NC):
            ti = c // 4
            par = c % 2
            if c % 4 == 0:
                ev_btld[0] = p.dma("gpsimd", s["Btt"], d["Bscr"][ti],
                                   waits=[ev_sch.get("bt_use"), ev_btld[0]], ch="bld")
                ev_btld[1] = p.dma("gpsimd", s["Ctt"], d["Cscr"][ti],
                                   waits=[ev_btld[1] if ti else None, ev_btld[0],
                                          ev_sch.get("ct_use"), ev_sch.get("bt_use")],
                                   ch="cld")
            ev_xld[par] = p.dma("sync", s["xg"][par], d["xscr"][c],
                                waits=[ev_xld[par], ev_sch.get(("xuse", par))], ch=f"x2{par}")
            ev_zld[par] = p.dma("sync", s["zg"][par], d["zscr"][c],
                                waits=[ev_zld[par], ev_sch.get(("zuse", par))], ch=f"z2{par}")
            ev_sl = p.dma("sync", s["Sl"], d["Sscr"][c], waits=[ev_sch.get("sluse"), ev_sl],
                          ch="sl")
            ev_ah = p.dma("sync", s["ahm"].rearrange("o (h i) -> o h i", h=32),
                          d["ascr"].rearrange("h (c i) -> c h i", i=128)[c:c + 1, :, :],
                          waits=[ev_sch.get("ahm")], ch="ah")
            ev_ch = p.dma("sync", s["chm"].rearrange("o (h i) -> o h i", h=32),
                          d["caprf"].rearrange("h (c i) -> c h i", i=128)[c:c + 1, :, :],
                          waits=[ev_sch.get("chm")], ch="chh")
            x3 = s["xg"][par].rearrange("p (h e) -> p h e", h=32)
            vt3 = s["SCLT_tm"][:, c * 128: c * 128 + 32].rearrange(
                "p (h o) -> p h o", o=1).to_broadcast([128, 32, 64])
            ev_xtl = p.op("vector", lambda v, x3=x3, vt3=vt3: v.tensor_tensor(
                s["xtl"].rearrange("p (h e) -> p h e", h=32), x3, vt3, ALU.mult),
                waits=[ev_xld[par], ev_in, ev_sch.get("xtluse")], inc="sV")
            ev_sch[("xuse", par)] = ev_xtl
            for h in range(32):
                pe = h % 2
                ba = pe
                bg = 2 + pe
                bY = 4 + pe
                bE = 6 + pe
                if c > 0:
                    es0 = p.op("vector", lambda v, h=h, c=c: v.tensor_scalar(
                        s["S0s"][:, h * 64:(h + 1) * 64], s["S0s"][:, h * 64:(h + 1) * 64],
                        s["eqc"][:, h * 16 + c - 1: h * 16 + c], None, ALU.mult),
                        waits=[ev_sch["s0"], ev_in], inc="sV")
                else:
                    es0 = ev_sch["s0"]
                esg = p.op("vector", lambda v, h=h, pe=pe: v.tensor_tensor(
                    s["Sgb"][pe], s["Sl"][:, h * 64:(h + 1) * 64],
                    s["S0s"][:, h * 64:(h + 1) * 64], ALU.add),
                    waits=[es0, ev_sl, ev_sch.get(("sgb", pe))], inc="sV")
                ev_sch["s0"] = esg
                ev_sch["sluse"] = esg
                ma = p.op("tensor", lambda t, h=h, ba=ba: t.matmul(
                    pool[ba][:, 0:128], s["onesf_row"], s["ahm"][0:1, h * 128:(h + 1) * 128],
                    start=True, stop=True),
                    waits=[ev_ah, ev_sch[ba], ev_ms], inc="sT")
                ev_sch["ahm"] = ma
                esc = p.op("vector", lambda v, ba=ba, pe=pe: v.tensor_tensor_scan(
                    s["Escr"][pe], pool[ba][:, 0:128], s["eye_f"], 0.0, ALU.mult, ALU.add),
                    waits=[ma, ev_sch.get(("Eu", pe)), ev_in], inc="sV")
                ev_sch[ba] = esc
                mcb = p.op("tensor", lambda t, h=h, bE=bE: t.matmul(
                    pool[bE][:, 0:128], s["onesf_row"], s["chm"][0:1, h * 128:(h + 1) * 128],
                    start=True, stop=True),
                    waits=[ev_ch, ev_sch[bE]], inc="sT")
                ev_sch["chm"] = mcb
                ecd = p.op("vector", lambda v, h=h, c=c, pe=pe, bE=bE: v.tensor_tensor(
                    s["Cde"][pe], s["Ctt"][:, h * 512 + (c % 4) * 128: h * 512 + (c % 4) * 128 + 128],
                    pool[bE][:, 0:128], ALU.mult),
                    waits=[mcb, ev_btld[1], ev_sch.get(("cde", pe))], inc="sV")
                ev_sch[bE] = ecd
                ev_sch["ct_use"] = ecd
                mg = p.op("tensor", lambda t, h=h, c=c, bg=bg: t.matmul(
                    pool[bg][:, 0:128],
                    s["Btt"][:, h * 512 + (c % 4) * 128: h * 512 + (c % 4) * 128 + 128],
                    s["Ctt"][:, h * 512 + (c % 4) * 128: h * 512 + (c % 4) * 128 + 128],
                    start=True, stop=True),
                    waits=[ev_btld[0], ev_btld[1], ev_sch[bg]], inc="sT")
                ev_sch["bt_use"] = mg
                em = p.op("vector", lambda v, h=h, c=c, pe=pe: v.scalar_tensor_tensor(
                    s["Mp"][pe], s["eye_bf"], s["SCLT_tm"][:, c * 128 + 32 + h: c * 128 + 33 + h],
                    s["Escr"][pe], ALU.mult, ALU.add),
                    waits=[esc, ev_inG, ev_in, ev_sch.get(("mp", pe))], inc="sV")
                ev_sch[("Eu", pe)] = em
                emp = p.op("vector", lambda v, pe=pe, bg=bg: v.tensor_tensor(
                    s["Mp"][pe], pool[bg][:, 0:128], s["Mp"][pe], ALU.mult),
                    waits=[em, mg], inc="sV")
                ev_sch[bg] = emp
                m2_ = p.op("tensor", lambda t, pe=pe, bY=bY: t.matmul(
                    pool[bY][:, 0:64], s["Cde"][pe], s["Sgb"][pe],
                    start=True, stop=False, skip_group_check=True),
                    waits=[ecd, esg, ev_sch[bY]], inc="sT")
                ev_sch[("cde", pe)] = m2_
                ev_sch[("sgb", pe)] = m2_
                m1_ = p.op("tensor", lambda t, h=h, pe=pe, bY=bY: t.matmul(
                    pool[bY][:, 0:64], s["Mp"][pe], s["xtl"][:, h * 64:(h + 1) * 64],
                    start=False, stop=True, skip_group_check=True),
                    waits=[emp, ev_xtl], inc="sT")
                ev_sch[("mp", pe)] = m1_
                ev_sch["xtluse"] = m1_
                eyv = p.op("vector", lambda v, h=h, par=par, bY=bY: v.scalar_tensor_tensor(
                    s["yl"][:, h * 64:(h + 1) * 64], s["xg"][par][:, h * 64:(h + 1) * 64],
                    s["Db"][:, h:h + 1], pool[bY][:, 0:64], ALU.mult, ALU.add),
                    waits=[m1_, evDb, ev_xld[par], ev_sch.get("ylw")], inc="sV")
                ev_sch[bY] = eyv
                ev_yl_done = eyv
            z3 = s["zg"][par]
            eg = p.op("vector", lambda v, z3=z3: v.tensor_tensor(
                s["yl"], s["yl"], z3, ALU.mult),
                waits=[ev_yl_done, ev_zld[par]], inc="sV")
            ev_sch[("zuse", par)] = eg
            ev_sch["ylw"] = eg
            evtr = None
            for k in range(16):
                bk = k % 2
                mt = p.op("tensor", lambda t, k=k, bk=bk: t.matmul(
                    pool[bk].bitcast(BF16)[:, 0:128], s["yl"][:, k * 128:(k + 1) * 128],
                    s["eye_bf"], start=True, stop=True, is_transpose=True),
                    waits=[eg, ev_sch[bk], ev_sch.get(("tr", bk)), ev_yfm_use], inc="sT")
                evtr = p.op("vector", lambda v, k=k, bk=bk: v.tensor_copy(
                    s["yfm"][:, k * 128:(k + 1) * 128], pool[bk].bitcast(BF16)[:, 0:128]),
                    waits=[mt], inc="sV")
                ev_sch[("tr", bk)] = evtr
                ev_sch[bk] = evtr
                ev_sch["ylw"] = evtr
            for dmc in range(NCH):
                bk = dmc
                opar = dmc % 2
                mm = None
                for k in range(16):
                    mm = p.op("tensor", lambda t, k=k, dmc=dmc, bk=bk: t.matmul(
                        pool[bk][:, 0:128], s["Wo"][:, k * 1024 + dmc * 128: k * 1024 + dmc * 128 + 128],
                        s["yfm"][:, k * 128:(k + 1) * 128],
                        start=(k == 0), stop=(k == 15)),
                        waits=([evtr, ev_inG, ev_sch[bk]] if k == 0 else []), inc="sT")
                ev_yfm_use = mm
                xrb = s["xres"][:, opar * 128:(opar + 1) * 128]
                evx = p.dma("sync", xrb, d["xT"][dmc * 128:(dmc + 1) * 128, c * 128:(c + 1) * 128],
                            waits=[ev_ost[opar]], ch=f"xr{opar}")
                ocb = s["ocb"][:, opar * 128:(opar + 1) * 128]
                eo = p.op("vector", lambda v, ocb=ocb, xrb=xrb, bk=bk: v.tensor_tensor(
                    ocb, pool[bk][:, 0:128], xrb, ALU.add),
                    waits=[mm, evx, ev_ost[opar]], inc="sV")
                ev_sch[bk] = eo
                evs = p.dma("sync", d["yT"][dmc * 128:(dmc + 1) * 128, c * 128:(c + 1) * 128],
                            ocb, waits=[eo], ch=f"o{opar}")
                ev_ost[opar] = evs
        block = ctx.enter_context(nc.Block())
        p.finish(block)
    return nc


def build_mamba0_program():
    nc = bass.Bass("TRN2", target_bir_lowering=False)
    d = {}
    d["xT"] = nc.dram_tensor("xT", [D_MODEL, T_CORE], F32, kind="ExternalInput").ap()
    d["W_in"] = nc.dram_tensor("W_in", [D_MODEL, 4480], F32, kind="ExternalInput").ap()
    d["dtb"] = nc.dram_tensor("dtb", [32, 1], F32, kind="ExternalInput").ap()
    d["selg"] = nc.dram_tensor("selg", [32, 1024], F32, kind="ExternalInput").ap()
    d["rep4"] = nc.dram_tensor("rep4", [32, 128], F32, kind="ExternalInput").ap()
    d["Th"] = nc.dram_tensor("Th", [128, 8], F32, kind="ExternalOutput").ap()
    with ExitStack() as ctx:
        p = Prog(nc, ctx)
        pool = [ctx.enter_context(nc.psum_tensor(f"pp{i}", [128, 512], F32)) for i in range(8)]
        mq = lambda name, shape, dt=F32: ctx.enter_context(nc.sbuf_tensor(name, shape, dt))[:, :]
        s = {}
        arena = mq("arena0", [128, 32768], BF16)
        s["X"] = arena.bitcast(F32)[:, 0:NCH * T_CORE]
        s["Xn"] = mq("Xn0", [128, 16384], BF16)
        s["Wbig"] = mq("Wb0", [128, 1024], BF16)
        s["SCLR"] = mq("SCLR0", [128, 2048])
        s["ANGQ"] = mq("ANGQ0", [128, 2048])
        s["DERIV"] = mq("DERIV0", [128, 2048])
        s["SCLT"] = mq("SCLT0", [128, 2048])
        s["sq2"] = mq("sq20", [128, 1024])
        s["xnt"] = mq("xnt0", [128, 512])
        s["stats"] = mq("stats0", [1, 2048])
        s["ones_col"] = mq("oc0", [128, 1])
        s["ones_row"] = mq("or0", [1, 128])
        s["onesf_row"] = mq("ofr0", [1, 128])
        s["zeros64"] = mq("z640", [64, 128])
        s["dtb"] = mq("dtb0", [32, 1])
        s["selg"] = mq("selg0", [32, 1024])
        s["rep4"] = mq("rep40", [32, 128])
        s["angrep"] = mq("ar0", [128, 512])
        s["useg"] = mq("us0", [128, 512])
        s["ThS"] = mq("ThS", [128, 8])
        s["thr"] = mq("thr0", [128, 1])
        cp = ConstPool(nc, ctx, p)
        p.op("vector", lambda v: v.memset(s["ones_col"], 1.0), inc="sV")
        p.op("vector", lambda v: v.memset(s["ones_row"], 1.0), inc="sV")
        p.op("vector", lambda v: v.memset(s["onesf_row"], 1.0), inc="sV")
        p.op("vector", lambda v: v.memset(s["zeros64"], 0.0), inc="sV")
        ev_th0 = p.op("vector", lambda v: v.memset(s["ThS"], 0.0), inc="sV")
        ev_ms = ev_th0
        s["ev_memset"] = ev_ms
        for c in range(NCH):
            p.dma("sync", s["X"][:, c * T_CORE:(c + 1) * T_CORE],
                  d["xT"][c * 128:(c + 1) * 128, :], ch="in")
        p.dma("sync", s["dtb"], d["dtb"], ch="in")
        p.dma("sync", s["selg"], d["selg"], ch="in")
        p.dma("sync", s["rep4"], d["rep4"], ch="in")
        ev_in = p.total("in")
        s["ev_dtb"] = ev_in
        s["ev_consts"] = ev_in
        ev_c = cp.get(LN_EPS)
        cp.get(1.0)
        X, Xn = s["X"], s["Xn"]
        ev_ln = emit_ln(p, pool, (0, 1, 2, 3), X, Xn, s["sq2"], s["xnt"], s["ones_col"],
                        s["ones_row"], s["stats"], T_CORE, [ev_c, ev_in, ev_ms])
        wscl = s["Wbig"]
        ev_w = p.dma("gpsimd", wscl.rearrange("p (c k) -> p c k", c=NCH),
                     d["W_in"].rearrange("(c p) f -> p c f", p=128)[:, :, 4352:4480], ch="wscl")
        ev_scl = None
        ev_ang = None
        for ti in range(4):
            t0 = ti * 512
            m1 = m2 = None
            for dc in range(NCH):
                m1 = p.op("tensor", lambda t, dc=dc, t0=t0: t.matmul(
                    pool[4][0:96, :], wscl[:, dc * 128: dc * 128 + 96],
                    Xn[:, dc * T_CORE + t0: dc * T_CORE + t0 + 512],
                    start=(dc == 0), stop=(dc == NCH - 1)),
                    waits=([ev_w, ev_ln[ti]] + ([ev_scl] if ev_scl else [])) if dc == 0 else [],
                    inc="sT")
            for dc in range(NCH):
                m2 = p.op("tensor", lambda t, dc=dc, t0=t0: t.matmul(
                    pool[5][0:32, :], wscl[:, dc * 128 + 96: dc * 128 + 128],
                    Xn[:, dc * T_CORE + t0: dc * T_CORE + t0 + 512],
                    start=(dc == 0), stop=(dc == NCH - 1)),
                    waits=([ev_ang] if ev_ang and dc == 0 else []), inc="sT")
            ev_scl = p.op("vector", lambda v, t0=t0: v.tensor_copy(
                s["SCLR"][0:96, t0:t0 + 512], pool[4][0:96, :]), waits=[m1], inc="sV")
            ev_ang = p.op("vector", lambda v, t0=t0: v.tensor_copy(
                s["ANGQ"][0:32, t0:t0 + 512], pool[5][0:32, :]), waits=[m2], inc="sV")
        ev = mamba_scalar_math(p, s, cp, ev_scl, ev_ang)
        for ti in range(4):
            t0 = ti * 512
            ang = s["ANGQ"][0:32, t0:t0 + 512]
            dt = s["DERIV"][0:32, t0:t0 + 512]
            evm1 = p.op("tensor", lambda t, t0=t0, ang=ang: t.matmul(
                pool[6][:, :], s["rep4"], ang, start=True, stop=True),
                waits=[ev["a"], ev_in], inc="sT")
            eva = p.op("vector", lambda v: v.tensor_copy(s["angrep"], pool[6][:, :]),
                       waits=[evm1], inc="sV")
            last_eu = None
            for g in range(8):
                mg = p.op("tensor", lambda t, g=g, dt=dt: t.matmul(
                    pool[7][:, :], s["selg"][:, g * 128:(g + 1) * 128], dt,
                    start=True, stop=True), waits=[eva, last_eu], inc="sT")
                eu = p.op("vector", lambda v: v.tensor_tensor(
                    s["useg"], s["angrep"], pool[7][:, :], ALU.mult),
                    waits=[mg], inc="sV")
                last_eu = eu
                er = p.op("vector", lambda v: v.tensor_reduce(
                    s["thr"], s["useg"], AX.X, ALU.add), waits=[eu], inc="sV")
                ev_th0 = p.op("vector", lambda v, g=g: v.tensor_tensor(
                    s["ThS"][:, g:g + 1], s["ThS"][:, g:g + 1], s["thr"], ALU.add),
                    waits=[er, ev_th0], inc="sV")
        p.dma("sync", d["Th"], s["ThS"], waits=[ev_th0], ch="out")
        block = ctx.enter_context(nc.Block())
        p.finish(block)
    return nc


def mamba_consts():
    selg = np.zeros((32, 8 * 128), np.float32)
    for g in range(8):
        for p_ in range(128):
            selg[4 * g + p_ // 32, g * 128 + p_] = 1.0
    rep4 = np.zeros((32, 128), np.float32)
    for p_ in range(128):
        rep4[p_ % 32, p_] = 1.0
    sel96 = np.zeros((128, 32), np.float32)
    for i in range(32):
        sel96[96 + i, i] = 1.0
    return selg, rep4, sel96


def mamba_maps(xs_list, W, core):
    selg, rep4, sel96 = mamba_consts()
    base = dict(
        xT=xs_list[core],
        W_in=np.ascontiguousarray(W["W_in"]),
        dtb=np.ascontiguousarray(W["dt_bias"].reshape(32, 1)),
        selg=selg, rep4=rep4,
    )
    m0 = dict(base)
    m1 = dict(base)
    m1["BbT"] = np.ascontiguousarray(
        np.concatenate([W["B_bias"].T, W["C_bias"].T], axis=1))
    m1["Bw"] = np.ascontiguousarray(
        np.stack([W["Bn_w"], W["Cn_w"]], axis=1))
    m1["sel96"] = sel96
    m1["identD"] = np.eye(128, dtype=np.float32)
    m1["thmask"] = None
    m1["ThAll"] = None
    return m0, m1


_NC_CACHE = {}


def _get(name, builder, *a):
    if name not in _NC_CACHE:
        _NC_CACHE[name] = builder(*a)
    return _NC_CACHE[name]


def kernel(**inp):
    import ml_dtypes
    x = np.asarray(inp["L0_attn_u"], np.float32)
    xs = shard_x(x)
    selg, rep4, sel96 = mamba_consts()
    eye = np.eye(128, dtype=np.float32)
    zB = np.zeros((1, 4096), ml_dtypes.bfloat16)
    zX = np.zeros((1, 2048), ml_dtypes.bfloat16)
    cores = list(range(8))

    moeW = dict(Wg=inp["L0_ffn_Wg"], bg=inp["L0_ffn_bg"], W1=inp["L0_ffn_W1"],
                b1=inp["L0_ffn_b1"], W2=inp["L0_ffn_W2"], b2=inp["L0_ffn_b2"])

    def run_moe(xs):
        nc = _get("moe", build_moe_program, 0.5)
        maps = [moe_in_map(xs[c], moeW) for c in range(8)]
        r = run_bass_kernel_spmd(nc, maps, core_ids=cores)
        return [np.asarray(r.results[c]["yT"]) for c in range(8)]

    def run_mamba(xs):
        W = dict(W_in=inp["L0_attn_W_in"], dt_bias=inp["L0_attn_dt_bias"],
                 B_bias=inp["L0_attn_B_bias"], C_bias=inp["L0_attn_C_bias"],
                 Bn_w=inp["L0_attn_Bn_w"], Cn_w=inp["L0_attn_Cn_w"],
                 D_param=inp["L0_attn_D_param"], W_out=inp["L0_attn_W_out"])
        base = lambda c: dict(xT=xs[c], W_in=np.ascontiguousarray(W["W_in"]),
                              dtb=np.ascontiguousarray(np.asarray(W["dt_bias"]).reshape(32, 1)),
                              selg=selg, rep4=rep4)
        nc0 = _get("m0", build_mamba0_program)
        r0 = run_bass_kernel_spmd(nc0, [base(c) for c in cores], core_ids=cores)
        Th = [np.asarray(r0.results[c]["Th"]) for c in range(8)]
        nc1 = _get("m1", build_mamba1_program)
        maps1 = []
        for c in range(8):
            b, q = c // 4, c % 4
            m = base(c)
            m["BbT"] = np.ascontiguousarray(np.concatenate(
                [np.asarray(W["B_bias"]).T, np.asarray(W["C_bias"]).T], axis=1))
            m["Bw"] = np.ascontiguousarray(np.stack([W["Bn_w"], W["Cn_w"]], axis=1))
            m["sel96"] = sel96
            m["identD"] = eye
            m["ThAll"] = np.stack([Th[b * 4 + j] for j in range(4)])
            tm = np.zeros((128, 4), np.float32)
            tm[:, :q] = 1.0
            m["thmask"] = tm
            maps1.append(m)
        r1 = run_bass_kernel_spmd(nc1, maps1, core_ids=cores)
        shp = dict(xscr=(16, 128, 2048), zscr=(16, 128, 2048), Bscr=(4, 128, 16384),
                   Cscr=(4, 128, 16384), Sscr=(16, 128, 2048))
        keys = ["xscr", "zscr", "Bscr", "Cscr", "Sscr", "ascr", "caprf", "cescr",
                "scal2", "sctm", "pubS", "pubBx", "pubXl"]
        scrs = []
        for c in range(8):
            sc_ = {}
            for k in keys:
                v = np.asarray(r1.results[c][k])
                sc_[k] = v.reshape(shp[k]) if k in shp else v
            scrs.append(sc_)
        nc2 = _get("m2", build_mamba2_program)
        maps2 = []
        for c in range(8):
            b, q = c // 4, c % 4
            qm = np.zeros((32, 4), np.float32)
            qm[:, :q] = 1.0
            maps2.append(dict(
                xT=xs[c], W_out=np.ascontiguousarray(W["W_out"]),
                Drow=np.ascontiguousarray(np.asarray(W["D_param"]).reshape(1, 32)),
                identD=eye,
                xscr=scrs[c]["xscr"], zscr=scrs[c]["zscr"], Bscr=scrs[c]["Bscr"],
                Cscr=scrs[c]["Cscr"], Sscr=scrs[c]["Sscr"], ascr=scrs[c]["ascr"],
                caprf=scrs[c]["caprf"], cescr=scrs[c]["cescr"], scal2=scrs[c]["scal2"],
                sctm=scrs[c]["sctm"],
                SAll=np.stack([scrs[b * 4 + j]["pubS"] for j in range(4)]),
                QtotAll=np.ascontiguousarray(np.stack(
                    [scrs[b * 4 + j]["cescr"][:, 31] for j in range(4)], axis=1)).astype(np.float32),
                qmask=qm, qmask1m=np.ascontiguousarray(1.0 - qm),
                prevB=np.ascontiguousarray(scrs[c - 1]["pubBx"].reshape(1, 4096)) if q > 0 else zB,
                prevXl=scrs[c - 1]["pubXl"] if q > 0 else zX,
            ))
        r2 = run_bass_kernel_spmd(nc2, maps2, core_ids=cores)
        return [np.asarray(r2.results[c]["yT"]) for c in range(8)]

    def run_tern(xs):
        W = {k: inp["L1_ffn_" + k] for k in
             ["w1_w", "w1_s", "w2_w", "w2_s", "h1_w", "h1_s", "h2_w", "h2_s",
              "w3_w", "w3_s"]}
        nc = _get("tern", build_ternary_program, 0.5)
        maps = [tern_in_map(xs[c], W) for c in range(8)]
        r = run_bass_kernel_spmd(nc, maps, core_ids=cores)
        return [np.asarray(r.results[c]["yT"]) for c in range(8)]

    def run_attn(xs):
        W = dict(Wqkv=inp["L1_attn_Wqkv"], Wproj=inp["L1_attn_Wproj"],
                 bproj=inp["L1_attn_bproj"])
        nc = _get("attn", build_attn_program)
        maps = [attn_in_map(xs[c], W) for c in range(8)]
        r = run_bass_kernel_spmd(nc, maps, core_ids=cores)
        return [np.asarray(r.results[c]["yT"]) for c in range(8)]

    xs = run_moe(xs)
    xs = run_mamba(xs)
    xs = run_moe(xs)
    xs = run_tern(xs)
    xs = run_attn(xs)
    xs = run_tern(xs)
    return unshard_x(xs).astype(np.float32)
